# version 23
# baseline (speedup 1.0000x reference)
import math
from contextlib import ExitStack

import numpy as np
import ml_dtypes

import concourse.bass as bass
import concourse.mybir as mybir
from concourse.bass_utils import run_bass_kernel_spmd

F32 = mybir.dt.float32
BF16 = mybir.dt.bfloat16
AF = mybir.ActivationFunctionType
ALU = mybir.AluOpType

B, S, D, H, DK, I = 2, 4096, 2048, 16, 128, 5632
DC = D // 128
IB = I // 128
NQ = 1024
NT = NQ // 512
SCALE = 1.0 / math.sqrt(DK)
EPS = 1e-5
KT_A, KT_B = 16, 32


class Sem:
    def __init__(self, h):
        self.h = h
        self.n = 0

    def inc(self, k=1):
        self.n += k
        return self.n


class Prog:

    def __init__(self, nc, block):
        self.nc = nc
        self.block = block
        self.q = {k: [] for k in "tvsgy"}
        self.dsem = {}
        self.dma_sems = {}
        self.sem_alloc = None

    def dat(self, eng, f):
        s = self.dsem[eng]
        if eng in "vsg" and s.n > 0:
            self.wt(eng, s, s.n)
        tgt = s.inc(1)
        self.q[eng].append(lambda en, f_=f, h=s.h: f_(en).then_inc(h, 1))
        return tgt

    def dsema(self, key):
        if key not in self.dma_sems:
            self.dma_sems[key] = Sem(self.sem_alloc("sd_" + key))
        return self.dma_sems[key]

    def dma(self, eng, key, f):
        s = self.dsema(key)
        tgt = s.inc(16)
        self.q[eng].append(lambda en, f_=f, h=s.h: f_(en).then_inc(h, 16))
        return (s, tgt)

    def wtd(self, eng, st):
        s, t = st
        self.wt(eng, s, t)

    def wt(self, eng, sem, val):
        if val > 0:
            self.q[eng].append(lambda en, h=sem.h, v=val: en.wait_ge(h, v))

    def flush(self):
        q, self.q = self.q, {k: [] for k in "tvsgy"}
        if q["y"]:
            self.block.sync(lambda e, fs=tuple(q["y"]): [f(e) for f in fs])
        if q["g"]:
            self.block.gpsimd(lambda e, fs=tuple(q["g"]): [f(e) for f in fs])
        if q["t"]:
            self.block.tensor(lambda e, fs=tuple(q["t"]): [f(e) for f in fs])
        if q["s"]:
            self.block.scalar(lambda e, fs=tuple(q["s"]): [f(e) for f in fs])
        if q["v"]:
            self.block.vector(lambda e, fs=tuple(q["v"]): [f(e) for f in fs])

    def barrier(self):
        sems = list(self.dsem.values()) + list(self.dma_sems.values())
        for eng in "ytsvg":
            for s in sems:
                self.wt(eng, s, s.n)
        self.flush()


def bcast_ap(t, nrep, ncols):
    return bass.AP(t, 0, [[ncols, 128], [0, nrep], [1, ncols]])


def build_nc():
    nc = bass.Bass("TRN2", target_bir_lowering=False)

    xq_h = nc.dram_tensor("xq", [128, DC, NQ], F32, kind="ExternalInput").ap()
    xctx_h = nc.dram_tensor("xctx", [128, DC, S], F32, kind="ExternalInput").ap()
    wq_h = nc.dram_tensor("wq", [128, DC, H, 128], F32, kind="ExternalInput").ap()
    wk_h = nc.dram_tensor("wk", [128, DC, H, 128], F32, kind="ExternalInput").ap()
    wv_h = nc.dram_tensor("wv", [128, DC, D], F32, kind="ExternalInput").ap()
    wo_h = nc.dram_tensor("wo", [DC, 128, DC, 128], F32, kind="ExternalInput").ap()
    wg_h = nc.dram_tensor("wg", [IB, 128, DC, 128], F32, kind="ExternalInput").ap()
    wu_h = nc.dram_tensor("wu", [IB, 128, DC, 128], F32, kind="ExternalInput").ap()
    wd_h = nc.dram_tensor("wd", [DC, 128, IB, 128], F32, kind="ExternalInput").ap()
    gs_h = nc.dram_tensor("gs", [128, IB], F32, kind="ExternalInput").ap()
    us_h = nc.dram_tensor("us", [128, IB], F32, kind="ExternalInput").ap()
    ds_h = nc.dram_tensor("ds", [128, DC], F32, kind="ExternalInput").ap()
    rqc_h = nc.dram_tensor("rqc", [64, NQ], F32, kind="ExternalInput").ap()
    rqs_h = nc.dram_tensor("rqs", [64, NQ], F32, kind="ExternalInput").ap()
    rkc_h = nc.dram_tensor("rkc", [64, S], F32, kind="ExternalInput").ap()
    rks_h = nc.dram_tensor("rks", [64, S], F32, kind="ExternalInput").ap()
    mask_h = nc.dram_tensor("mask", [128, 16, 512], BF16, kind="ExternalInput").ap()
    out_h = nc.dram_tensor("out", [128, DC, NQ], F32, kind="ExternalOutput").ap()

    ln_scr = nc.dram_tensor("ln_scr", [128, DC, S + NQ], BF16).ap()
    k_scr = nc.dram_tensor("k_scr", [H, 128, S], BF16).ap()
    v_scr = nc.dram_tensor("v_scr", [128, S // 128, D], BF16).ap()
    x2_scr = nc.dram_tensor("x2_scr", [128, DC, NQ], F32).ap()

    es_top = ExitStack()
    with es_top:
        e = es_top.enter_context
        ones_c = e(nc.sbuf_tensor("ones_c", [128, 1], BF16))
        ones_cf = e(nc.sbuf_tensor("ones_cf", [128, 1], F32))
        epsb = e(nc.sbuf_tensor("epsb", [128, 1], F32))
        ones_r = e(nc.sbuf_tensor("ones_r", [1, 128], F32))
        st_ss2 = e(nc.sbuf_tensor("st_ss2", [1, 1024], F32))
        st_m = e(nc.sbuf_tensor("st_m", [1, 512], F32))
        st_q = e(nc.sbuf_tensor("st_q", [1, 512], F32))
        st_r = e(nc.sbuf_tensor("st_r", [1, 512], F32))
        st_b = e(nc.sbuf_tensor("st_b", [1, 512], F32))
        den_r = e(nc.sbuf_tensor("den_r", [1, 512], F32))
        rbc = e(nc.sbuf_tensor("rbc", [128, 512], F32))
        bbc = e(nc.sbuf_tensor("bbc", [128, 512], F32))
        gs_sb = e(nc.sbuf_tensor("gs_sb", [128, IB], F32))
        us_sb = e(nc.sbuf_tensor("us_sb", [128, IB], F32))
        ds_sb = e(nc.sbuf_tensor("ds_sb", [128, DC], F32))
        psA = e(nc.psum_tensor("psA", [128, 1024], F32))
        psB = e(nc.psum_tensor("psB", [128, 1024], F32))
        psC = e(nc.psum_tensor("psC", [128, 512], F32))
        psD = e(nc.psum_tensor("psD", [128, 512], F32))
        psE = e(nc.psum_tensor("psE", [1, 1024], F32))
        sT = Sem(e(nc.semaphore("sT")))
        sV = Sem(e(nc.semaphore("sV")))
        sS = Sem(e(nc.semaphore("sS")))
        sG = Sem(e(nc.semaphore("sG")))
        block = e(nc.Block())
        p = Prog(nc, block)
        p.dsem = {"t": sT, "v": sV, "s": sS, "g": sG}
        p.sem_alloc = lambda name: e(nc.semaphore(name))

        def add_mm(ps_ap, lhsT, rhs, start, stop):
            return p.dat("t", lambda en, o=ps_ap, l=lhsT, r=rhs, a=start, z=stop:
                         en.matmul(o, l, r, start=a, stop=z))

        def stats_math(from_sb=None):
            if from_sb is None:
                p.wt("v", sT, sT.n)
                s1ap, s2ap = psE[:, 0:512], psE[:, 512:1024]
            else:
                s1ap, s2ap = from_sb[:, 0:512], from_sb[:, 512:1024]
            p.dat("v", lambda en, a=s1ap: en.tensor_scalar_mul(st_m[:], a, 1.0 / D))
            p.dat("v", lambda en: en.scalar_tensor_tensor(st_q[:], st_m[:], 1.0, st_m[:], ALU.mult, ALU.mult))
            vq = p.dat("v", lambda en, a=s2ap: en.scalar_tensor_tensor(
                st_q[:], a, 1.0 / D, st_q[:], ALU.mult, ALU.subtract))
            p.wt("s", sV, vq)
            sq = p.dat("s", lambda en: en.activation(st_q[:], st_q[:], AF.Sqrt, bias=epsb[0:1, :]))
            p.wt("v", sS, sq)
            p.dat("v", lambda en: en.reciprocal(st_r[:], st_q[:]))
            return p.dat("v", lambda en: en.scalar_tensor_tensor(st_b[:], st_m[:], -1.0, st_r[:], ALU.mult, ALU.mult))

        def bcast_rb(v_ready):
            p.wt("t", sV, v_ready)
            p.dat("t", lambda en: en.matmul(psC[:], ones_r[:], st_r[:], start=True, stop=True))
            t2 = p.dat("t", lambda en: en.matmul(psD[:], ones_r[:], st_b[:], start=True, stop=True))
            p.wt("s", sT, t2)
            p.dat("s", lambda en: en.copy(rbc[:], psC[:]))
            return p.dat("s", lambda en: en.copy(bbc[:], psD[:]))

        def emit_ln_chunk(src_f32, dst_bf16, xb16, xsq):
            p.dat("s", lambda en, s_=src_f32: en.copy(xb16[:], s_))
            s0 = p.dat("s", lambda en, s_=src_f32: en.activation(xsq[:], s_, AF.Square))
            p.wt("t", sS, s0)
            for dc in range(DC):
                add_mm(psE[:, 0:512], ones_c[:], xb16[:, dc, :], dc == 0, dc == DC - 1)
            for dc in range(DC):
                add_mm(psE[:, 512:1024], ones_c[:], xsq[:, dc, :], dc == 0, dc == DC - 1)
            vr = stats_math()
            sr = bcast_rb(vr)
            p.wt("v", sS, sr)
            p.dat("v", lambda en, d_=dst_bf16, s_=src_f32: en.tensor_tensor(
                d_, s_, bcast_ap(rbc, DC, 512), ALU.mult))
            return p.dat("v", lambda en, d_=dst_bf16: en.tensor_tensor(
                d_, d_, bcast_ap(bbc, DC, 512), ALU.add))

        p.dat("g", lambda en: en.memset(ones_c[:], 1.0))
        p.dat("g", lambda en: en.memset(ones_cf[:], 1.0))
        p.dat("g", lambda en: en.memset(epsb[:], EPS))
        p.dat("g", lambda en: en.memset(ones_r[:], 1.0))
        p.dma("y", "const", lambda en: en.dma_start(out=gs_sb[:], in_=gs_h))
        p.dma("y", "const", lambda en: en.dma_start(out=us_sb[:], in_=us_h))
        p.dma("y", "const", lambda en: en.dma_start(out=ds_sb[:], in_=ds_h))
        p.barrier()

        with ExitStack() as ph:
            pe = ph.enter_context
            xa = [pe(nc.sbuf_tensor(f"xa{i}", [128, DC, 512], F32)) for i in range(4)]
            lno = [pe(nc.sbuf_tensor(f"lno{i}", [128, DC, 512], BF16)) for i in range(1)]
            xsq = [pe(nc.sbuf_tensor(f"xsq{i}", [128, DC, 512], BF16)) for i in range(2)]

            stats_tgt = {}
            pse_free = {}
            vr = {}
            t2b = {}
            cp_tgt = {}
            norm = {}
            store_done = {}
            sbank = [(psE[:, 0:512], psE[:, 512:1024]), (psA[0:1, 0:512], psA[0:1, 512:1024])]

            def s_prep(i):
                p.wtd("s", lds[i])
                if i >= 2:
                    p.wt("s", sT, stats_tgt[i - 2])
                s0 = p.dat("s", lambda en, b=xa[i % 4], q=xsq[i % 2]: en.activation(q[:], b[:], AF.Square))
                p.wtd("t", lds[i])
                if i >= 2:
                    p.wt("t", sV, pse_free[i - 2])
                b1, b2 = sbank[i % 2]
                for dc in range(DC):
                    add_mm(b1, ones_cf[:], xa[i % 4][:, dc, :], dc == 0, dc == DC - 1)
                p.wt("t", sS, s0)
                for dc in range(DC):
                    stats_tgt[i] = add_mm(b2, ones_c[:], xsq[i % 2][:, dc, :], dc == 0, dc == DC - 1)

            def math(i):
                b1, b2 = sbank[i % 2]
                p.wt("v", sT, stats_tgt[i])
                p.dat("v", lambda en, a=b1: en.tensor_scalar_mul(st_m[:], a, 1.0 / D))
                p.dat("v", lambda en: en.scalar_tensor_tensor(st_q[:], st_m[:], 1.0, st_m[:], ALU.mult, ALU.mult))
                pse_free[i] = p.dat("v", lambda en, a=b2: en.scalar_tensor_tensor(
                    st_q[:], a, 1.0 / D, st_q[:], ALU.mult, ALU.subtract))
                p.wt("s", sV, pse_free[i])
                sq = p.dat("s", lambda en: en.activation(st_q[:], st_q[:], AF.Sqrt, bias=epsb[0:1, :]))
                p.wt("v", sS, sq)
                if i >= 1:
                    p.wt("v", sT, t2b[i - 1])
                p.dat("v", lambda en: en.reciprocal(st_r[:], st_q[:]))
                vr[i] = p.dat("v", lambda en: en.scalar_tensor_tensor(st_b[:], st_m[:], -1.0, st_r[:], ALU.mult, ALU.mult))

            def finish(j):
                p.wt("t", sV, vr[j])
                if j >= 1:
                    p.wt("t", sS, cp_tgt[j - 1])
                p.dat("t", lambda en: en.matmul(psC[:], ones_r[:], st_r[:], start=True, stop=True))
                t2b[j] = p.dat("t", lambda en: en.matmul(psD[:], ones_r[:], st_b[:], start=True, stop=True))
                p.wt("s", sT, t2b[j])
                if j >= 1:
                    for sm_, ct_ in norm[j - 1]:
                        p.wt("s", sm_, ct_)
                p.dat("s", lambda en: en.copy(rbc[:], psC[:]))
                cp_tgt[j] = p.dat("s", lambda en: en.copy(bbc[:], psD[:]))
                eng = "v"
                p.wt(eng, sS, cp_tgt[j])
                srcb = xa[j % 4]
                if j >= 1:
                    p.wtd(eng, store_done[j - 1])
                p.dat(eng, lambda en, s_=srcb: en.tensor_tensor(
                    lno[0][:], s_[:], bcast_ap(rbc, DC, 512), ALU.mult))
                norm[j] = [(p.dsem[eng], p.dat(eng, lambda en: en.tensor_tensor(
                    lno[0][:], lno[0][:], bcast_ap(bbc, DC, 512), ALU.add)))]
                for sm_, ct_ in norm[j]:
                    p.wt("y", sm_, ct_)
                store_done[j] = p.dma("y", "lno0", lambda en, j_=j: en.dma_start(
                    out=ln_scr[:, :, j_ * 512:(j_ + 1) * 512], in_=lno[0][:]))

            lds = {}
            for i in range(10):
                if i >= 4:
                    for sm_, ct_ in norm[i - 4]:
                        p.wt("y", sm_, ct_)
                srch = (xctx_h[:, :, i * 512:(i + 1) * 512] if i < 8
                        else xq_h[:, :, (i - 8) * 512:(i - 7) * 512])
                lds[i] = p.dma("y", f"xa{i % 4}", lambda en, s=srch, b=xa[i % 4]: en.dma_start(out=b[:], in_=s))
                if i >= 2:
                    finish(i - 2)
                s_prep(i)
                if i >= 1:
                    math(i - 1)
            finish(8)
            math(9)
            finish(9)
            p.barrier()

        def emit_rope(eng, src_top, src_bot, dst_top, dst_bot, cos_ap, sin_ap, t1, t2, width):
            p.dat(eng, lambda en: en.tensor_tensor(t1[:, 0:width], src_top, cos_ap, ALU.mult))
            p.dat(eng, lambda en: en.tensor_tensor(t2[:, 0:width], src_bot, sin_ap, ALU.mult))
            p.dat(eng, lambda en: en.tensor_sub(dst_top, t1[:, 0:width], t2[:, 0:width]))
            p.dat(eng, lambda en: en.tensor_tensor(t1[:, 0:width], src_top, sin_ap, ALU.mult))
            p.dat(eng, lambda en: en.tensor_tensor(t2[:, 0:width], src_bot, cos_ap, ALU.mult))
            return (p.dsem[eng], p.dat(eng, lambda en: en.tensor_add(dst_bot, t1[:, 0:width], t2[:, 0:width])))

        with ExitStack() as ph:
            pe = ph.enter_context
            wk_sb = pe(nc.sbuf_tensor("wk_sb", [128, DC, H, 128], BF16))
            wv_sb = pe(nc.sbuf_tensor("wv_sb", [128, DC, D], BF16))
            lns = [pe(nc.sbuf_tensor(f"lns{i}", [128, DC, 512], BF16)) for i in range(2)]
            kout = [pe(nc.sbuf_tensor(f"kout{i}", [128, 512], BF16)) for i in range(4)]
            vout = [pe(nc.sbuf_tensor(f"vout{i}", [128, D], BF16)) for i in range(2)]
            rkc = [pe(nc.sbuf_tensor(f"rkc{i}", [64, 512], F32)) for i in range(2)]
            rks = [pe(nc.sbuf_tensor(f"rks{i}", [64, 512], F32)) for i in range(2)]
            tv1 = pe(nc.sbuf_tensor("tv1k", [64, 512], F32))
            tv2 = pe(nc.sbuf_tensor("tv2k", [64, 512], F32))

            kslot = [psA[:, 0:512], psA[:, 512:1024], psB[:, 0:512], psB[:, 512:1024]]
            wkl = [p.dma("g", f"wk{i}", lambda en, i_=i: en.dma_start(
                out=wk_sb[:, :, 4 * i_:4 * i_ + 4, :], in_=wk_h[:, :, 4 * i_:4 * i_ + 4, :]))
                   for i in range(4)]
            wvl = [p.dma("g", f"wv{i}", lambda en, i_=i: en.dma_start(
                out=wv_sb[:, :, 512 * i_:512 * (i_ + 1)], in_=wv_h[:, :, 512 * i_:512 * (i_ + 1)]))
                   for i in range(4)]
            ku = 0
            gidx = 0
            vu = 0
            rope_done = {}
            kst_done = {}
            ev_list = []
            vst_done = {}
            slice_mm_done = {}
            rop_ld = {}
            for s in range(8):
                if s >= 2:
                    p.wt("y", sT, slice_mm_done[s - 2])
                ld = p.dma("y", f"lnsk{s % 2}", lambda en, s_=s, b=lns[s % 2]: en.dma_start(
                    out=b[:], in_=ln_scr[:, :, s_ * 512:(s_ + 1) * 512]))
                if s >= 2:
                    p.wt("y", *rope_done.get((s - 2) * 16 + 15, (sV, 0)))
                p.dma("y", f"rk{s % 2}", lambda en, s_=s, b=rkc[s % 2]: en.dma_start(
                    out=b[:], in_=rkc_h[:, s_ * 512:(s_ + 1) * 512]))
                rop_ld[s] = p.dma("y", f"rk{s % 2}", lambda en, s_=s, b=rks[s % 2]: en.dma_start(
                    out=b[:], in_=rks_h[:, s_ * 512:(s_ + 1) * 512]))
                first = True

                def k_unit(h, s_=s):
                    nonlocal ku, first
                    ps = kslot[ku % 4]
                    if ku < 16:
                        p.wtd("t", wkl[h // 4])
                    if ku >= 4:
                        p.wt("t", *rope_done[ku - 4])
                    if first:
                        p.wtd("t", ld)
                        first = False
                    for dc in range(DC):
                        add_mm(ps, wk_sb[:, dc, h, :], lns[s_ % 2][:, dc, :],
                               dc == 0, dc == DC - 1)
                    mmt = sT.n
                    ko = kout[ku % 4]
                    p.wt("v", sT, mmt)
                    if ku == 0 or (ku % 16 == 0):
                        p.wtd("v", rop_ld[s_])
                    if ku >= 4:
                        p.wtd("v", kst_done[ku - 4])
                    rope_done[ku] = emit_rope("v", ps[0:64, :], ps[64:128, :],
                                              ko[0:64, :], ko[64:128, :],
                                              rkc[s_ % 2][:, :], rks[s_ % 2][:, :],
                                              tv1, tv2, 512)
                    p.wt("y", *rope_done[ku])
                    kst_done[ku] = p.dma("y", f"kout{ku % 4}", lambda en, h_=h, s2=s_, b=ko:
                                         en.dma_start(out=k_scr[h_, :, s2 * 512:(s2 + 1) * 512], in_=b[:]))
                    ku += 1

                def v_unit(tt, s_=s):
                    nonlocal vu, gidx, first
                    vo = vout[vu % 2]
                    for dvc in range(4):
                        ps = psC if gidx % 2 == 0 else psD
                        if vu < 4:
                            p.wtd("t", wvl[dvc])
                        if first:
                            p.wtd("t", ld)
                            first = False
                        if gidx >= 2:
                            p.wt("t", sS, ev_list[gidx - 2])
                        for dc in range(DC):
                            add_mm(ps[:], lns[s_ % 2][:, dc, tt * 128:(tt + 1) * 128],
                                   wv_sb[:, dc, dvc * 512:(dvc + 1) * 512],
                                   dc == 0, dc == DC - 1)
                        p.wt("s", sT, sT.n)
                        if vu >= 2 and dvc == 0:
                            p.wtd("s", vst_done[vu - 2])
                        ev_list.append(p.dat("s", lambda en, v_=dvc, ps_=ps, b=vo: en.copy(
                            b[:, v_ * 512:(v_ + 1) * 512], ps_[:])))
                        gidx += 1
                    p.wt("y", sS, ev_list[-1])
                    vst_done[vu] = p.dma("y", f"vout{vu % 2}", lambda en, s2=s_, t_=tt, b=vo:
                                         en.dma_start(out=v_scr[:, s2 * 4 + t_, :], in_=b[:]))
                    vu += 1

                if s == 0:
                    for h in range(H):
                        k_unit(h)
                    for blk in range(4):
                        v_unit(blk)
                else:
                    for blk in range(4):
                        for h in range(4 * blk, 4 * blk + 4):
                            k_unit(h)
                        v_unit(blk)
                slice_mm_done[s] = sT.n
            p.barrier()

        es_qT = ExitStack()
        qT = es_qT.enter_context(nc.sbuf_tensor("qT", [128, H, NQ], BF16))

        with ExitStack() as ph:
            pe = ph.enter_context
            wq_sb = pe(nc.sbuf_tensor("wq_sb", [128, DC, H, 128], BF16))
            lnq = [pe(nc.sbuf_tensor(f"lnq{i}", [128, DC, 512], BF16)) for i in range(2)]
            rqc = pe(nc.sbuf_tensor("rqc_s", [64, NQ], F32))
            rqs = pe(nc.sbuf_tensor("rqs_s", [64, NQ], F32))
            tv1 = pe(nc.sbuf_tensor("tv1", [64, 1024], F32))
            tv2 = pe(nc.sbuf_tensor("tv2", [64, 1024], F32))

            wql = [p.dma("g", f"wq{i}", lambda en, i_=i: en.dma_start(
                out=wq_sb[:, :, 4 * i_:4 * i_ + 4, :], in_=wq_h[:, :, 4 * i_:4 * i_ + 4, :]))
                   for i in range(4)]
            p.dma("y", "const", lambda en: en.dma_start(out=rqc[:], in_=rqc_h))
            rl = p.dma("y", "const", lambda en: en.dma_start(out=rqs[:], in_=rqs_h))
            lql = [p.dma("y", f"lnq{i}", lambda en, i_=i: en.dma_start(
                out=lnq[i_][:], in_=ln_scr[:, :, S + i_ * 512:S + (i_ + 1) * 512]))
                for i in range(2)]
            p.wtd("t", lql[0])
            p.wtd("t", lql[1])
            p.wtd("v", rl)
            rope_done = {}
            for h in range(H):
                ps = psA if h % 2 == 0 else psB
                p.wtd("t", wql[h // 4])
                if h >= 2:
                    p.wt("t", *rope_done[h - 2])
                for nt in range(NT):
                    for dc in range(DC):
                        add_mm(ps[:, nt * 512:(nt + 1) * 512], wq_sb[:, dc, h, :],
                               lnq[nt][:, dc, :], dc == 0, dc == DC - 1)
                p.wt("v", sT, sT.n)
                rope_done[h] = emit_rope("v", ps[0:64, 0:1024], ps[64:128, 0:1024],
                                         qT[0:64, h, :], qT[64:128, h, :],
                                         rqc[:, 0:1024], rqs[:, 0:1024], tv1, tv2, 1024)
            p.barrier()

        es_ctxT = ExitStack()
        ctxT = es_ctxT.enter_context(nc.sbuf_tensor("ctxT", [128, H, NQ], BF16, side="right"))
        with ExitStack() as ph:
            pe = ph.enter_context
            mask_sb = pe(nc.sbuf_tensor("mask_sb", [128, 16, 512], BF16))
            k_sb = [pe(nc.sbuf_tensor(f"ksb{i}", [128, S], BF16)) for i in range(2)]
            v_sb = [pe(nc.sbuf_tensor(f"vsb{i}", [128, S // 128, 256], BF16)) for i in range(2)]
            e_sb = [[pe(nc.sbuf_tensor(f"esb{s}{i}", [128, 1024], BF16)) for i in range(2)]
                    for s in range(2)]
            den_rs = [pe(nc.sbuf_tensor(f"denr{s}", [1, 512], F32)) for s in range(2)]

            ml = p.dma("y", "const", lambda en: en.dma_start(out=mask_sb[:], in_=mask_h))
            p.wtd("g", ml)
            p.wtd("v", ml)

            KT = [KT_A, KT_B]
            NP = [KT_A // 2, KT_B // 2]
            ctx_ps = [psC, psD]
            den_ap = [psE[:, 0:512], psE[:, 512:1024]]
            pslot = [psA, psB]
            pslot_free = [None, None]
            pair_ctr = 0
            bundle = 0
            pend = []
            hd = {}
            rbc_guard = [0]
            mask_eng = [0]
            vsb_ld = [None, None]
            prev_at = [(0, 0), (0, 0)]

            def flush_pend(force=False):
                while pend and (force or bundle >= pend[0]["after"]):
                    ev = pend.pop(0)
                    s_, h_, sl = ev["s"], ev["h"], ev["slot"]
                    p.wt("t", sV, ev["vr"])
                    if pslot_free[sl] is not None:
                        p.wt("t", *pslot_free[sl])
                    tb = p.dat("t", lambda en, ps_=pslot[sl], d=den_rs[s_]: en.matmul(
                        ps_[:, 0:512], ones_r[:], d[:], start=True, stop=True))
                    p.wt("s", sT, tb)
                    if rbc_guard[0]:
                        p.wt("s", sV, rbc_guard[0])
                    rc = p.dat("s", lambda en, ps_=pslot[sl]: en.copy(rbc[:], ps_[:, 0:512]))
                    pslot_free[sl] = (sS, rc)
                    p.wt("v", sS, rc)
                    hd[(h_, s_)] = p.dat("v", lambda en, h2=h_, s2=s_, cp=ctx_ps[s_]: en.tensor_mul(
                        ctxT[:, h2, s2 * 512:(s2 + 1) * 512], cp[:], rbc[:]))
                    rbc_guard[0] = hd[(h_, s_)]

            for h in range(H):
                if h % 2 == 0:
                    if h >= 4:
                        p.wt("y", sV, hd[(h - 3, 1)])
                    vsb_ld[(h // 2) % 2] = p.dma("y", f"vsb{(h // 2) % 2}",
                        lambda en, h_=h, b=v_sb[(h // 2) % 2]: en.dma_start(
                            out=b[:], in_=v_scr[:, :, h_ * 128:(h_ + 2) * 128]))
                if h >= 2:
                    p.wt("y", sV, hd[(h - 2, 1)])
                kv_ld = p.dma("y", f"ksb{h % 2}",
                              lambda en, h_=h, b=k_sb[h % 2]: en.dma_start(
                                  out=b[:], in_=k_scr[h_]))

                et = [{}, {}]
                gt_ = [{}, {}]
                at = [{}, {}]
                last_slot = [None, None]
                firsts = [True]

                def sc_pair(s, pr, h_=h, kvl=kv_ld):
                    nonlocal pair_ctr
                    sl = pair_ctr % 2
                    pair_ctr += 1
                    if pslot_free[sl] is not None:
                        p.wt("t", *pslot_free[sl])
                    if firsts[0]:
                        p.wtd("t", kvl)
                        p.wtd("t", vsb_ld[(h_ // 2) % 2])
                        firsts[0] = False
                    for half in (0, 1):
                        t = 2 * pr + half
                        tgt = add_mm(pslot[sl][:, half * 512:(half + 1) * 512],
                                     k_sb[h_ % 2][:, t * 128:(t + 1) * 128],
                                     qT[:, h_, s * 512:(s + 1) * 512], True, True)
                    last_slot[s] = sl
                    return sl, tgt

                def exp_pair(s, pr, sl, sc):
                    p.wt("s", sT, sc)
                    if pr >= 2:
                        p.wt("s", sT, at[s][pr - 2])
                    elif prev_at[s][pr]:
                        p.wt("s", sT, prev_at[s][pr])
                    eb = e_sb[s][pr % 2]
                    et[s][pr] = p.dat("s", lambda en, b=eb, ps_=pslot[sl]:
                                      en.activation(b[:], ps_[:], AF.Exp, scale=SCALE))
                    pslot_free[sl] = (sS, et[s][pr])
                    masked = (s == 0) or (pr >= 8)
                    if masked:
                        mi = 2 * (pr if s == 0 else pr - 8)
                        eng = "v" if mask_eng[0] == 0 else "g"
                        mask_eng[0] ^= 1
                        p.wt(eng, sS, et[s][pr])
                        mask_ap = bass.AP(mask_sb, mi * 512, [[16 * 512, 128], [1, 1024]])
                        gt_[s][pr] = (p.dsem[eng], p.dat(eng, lambda en, b=eb, m=mask_ap:
                                      en.tensor_mul(b[:], b[:], m)))
                    else:
                        gt_[s][pr] = (sS, et[s][pr])

                def av_pair(s, pr, h_=h):
                    gsem, gv = gt_[s][pr]
                    p.wt("t", gsem, gv)
                    if pr == 0 and h_ >= 1:
                        p.wt("t", sV, hd[(h_ - 1, s)])
                    eb = e_sb[s][pr % 2]
                    for half in (0, 1):
                        t = 2 * pr + half
                        vap = bass.AP(v_sb[(h_ // 2) % 2], t * 256 + (h_ % 2) * 128,
                                      [[(S // 128) * 256, 128], [1, 128]])
                        add_mm(ctx_ps[s][:], vap, eb[:, half * 512:(half + 1) * 512],
                               t == 0, t == KT[s] - 1)
                    for half in (0, 1):
                        t = 2 * pr + half
                        at[s][pr] = add_mm(den_ap[s], ones_c[:],
                                           eb[:, half * 512:(half + 1) * 512],
                                           t == 0, t == KT[s] - 1)

                def end_stream(s, h_=h):
                    p.wt("v", sT, at[s][NP[s] - 1])
                    vr = p.dat("v", lambda en, s_=s: en.reciprocal(den_rs[s_][:], den_ap[s_]))
                    pend.append({"s": s, "h": h_, "vr": vr,
                                 "slot": last_slot[s], "after": bundle + 2})

                order = []
                for i in range(8):
                    order += [(1, 2 * i), (0, i), (1, 2 * i + 1)]
                pnd = [None, None]
                for (s, pr) in order:
                    flush_pend()
                    sl, sc = sc_pair(s, pr)
                    exp_pair(s, pr, sl, sc)
                    if pnd[s] is not None:
                        av_pair(s, pnd[s])
                    pnd[s] = pr
                    if pr == NP[s] - 1:
                        av_pair(s, pr)
                        pnd[s] = None
                        end_stream(s)
                    bundle += 1
                prev_at[0] = (at[0].get(NP[0] - 2, 0), at[0].get(NP[0] - 1, 0))
                prev_at[1] = (at[1].get(NP[1] - 2, 0), at[1].get(NP[1] - 1, 0))
            flush_pend(force=True)
            p.barrier()

        es_qT.close()

        es_h2 = ExitStack()
        h2 = es_h2.enter_context(nc.sbuf_tensor("h2", [128, DC, NQ], BF16))
        with ExitStack() as ph:
            pe = ph.enter_context
            wos = [pe(nc.sbuf_tensor(f"wos{i}", [128, DC, 128], BF16)) for i in range(2)]
            v4 = pe(nc.sbuf_tensor("v4", [128, DC, 1024], BF16))
            vsq4 = pe(nc.sbuf_tensor("vsq4", [128, 1024], BF16))
            xqs = pe(nc.sbuf_tensor("xqs", [128, DC, 512], F32))
            x2c = pe(nc.sbuf_tensor("x2c", [128, DC, 512], F32))
            x2q8 = pe(nc.sbuf_tensor("x2q8", [128, 8, 512], BF16))

            evac_done = {}
            stat_done = {}
            mm_done = {}

            def wo_stats(j):
                p.wt("t", sS, evac_done[j])
                for nt in range(NT):
                    s1dst = psE[:, 0:512] if nt == 0 else psC[0:1, :]
                    s2dst = psE[:, 512:1024] if nt == 0 else psD[0:1, :]
                    add_mm(s1dst, ones_c[:], v4[:, j, nt * 512:(nt + 1) * 512],
                           j == 0, j == DC - 1)
                    stat_done[j] = add_mm(s2dst, ones_c[:],
                                          vsq4[:, nt * 512:(nt + 1) * 512],
                                          j == 0, j == DC - 1)

            for oc in range(DC):
                if oc >= 2:
                    p.wt("g", sS, evac_done[oc - 2])
                wl = p.dma("g", f"wos{oc % 2}", lambda en, o_=oc: en.dma_start(
                    out=wos[o_ % 2][:], in_=wo_h[o_]))
                p.wtd("t", wl)
                ps = psA if oc % 2 == 0 else psB
                if oc >= 2:
                    p.wt("t", sS, evac_done[oc - 2])
                for nt in range(NT):
                    for dvc in range(DC):
                        mm_done[oc] = add_mm(ps[:, nt * 512:(nt + 1) * 512],
                                             wos[oc % 2][:, dvc, :],
                                             ctxT[:, dvc, nt * 512:(nt + 1) * 512],
                                             dvc == 0, dvc == DC - 1)
                if oc >= 1:
                    wo_stats(oc - 1)
                    p.wt("s", sT, stat_done[oc - 1])
                else:
                    p.wt("s", sT, mm_done[oc])
                p.dat("s", lambda en, o_=oc, ps_=ps: en.copy(v4[:, o_, :], ps_[:]))
                evac_done[oc] = p.dat("s", lambda en, ps_=ps: en.activation(
                    vsq4[:], ps_[:], AF.Square))
            wo_stats(DC - 1)
            stats_ready = stat_done[DC - 1]

            p.wt("v", sT, stats_ready)
            p.dat("v", lambda en: en.tensor_copy(st_ss2[:, 0:512], psC[0:1, :]))
            vsave = p.dat("v", lambda en: en.tensor_copy(st_ss2[:, 512:1024], psD[0:1, :]))

            x2_store = {}
            x2q8_busy = [None]
            for nt in range(NT):
                if nt == 1:
                    p.wt("y", sV, sV.n)
                xl = p.dma("y", "xqld", lambda en, n_=nt: en.dma_start(
                    out=xqs[:], in_=xq_h[:, :, n_ * 512:(n_ + 1) * 512]))
                if nt == 0:
                    p.wt("t", sV, vsave)
                    vr = stats_math()
                else:
                    vr = stats_math(from_sb=st_ss2)
                sr = bcast_rb(vr)
                p.wt("v", sS, sr)
                p.wtd("v", xl)
                if nt == 1:
                    p.wtd("v", x2_store[0])
                p.dat("v", lambda en, n_=nt: en.tensor_tensor(
                    x2c[:], v4[:, :, n_ * 512:(n_ + 1) * 512], bcast_ap(rbc, DC, 512), ALU.mult))
                p.dat("v", lambda en: en.tensor_tensor(x2c[:], x2c[:], bcast_ap(bbc, DC, 512), ALU.add))
                vx2 = p.dat("v", lambda en: en.tensor_add(x2c[:], x2c[:], xqs[:]))
                p.wt("y", sV, vx2)
                x2_store[nt] = p.dma("y", "x2st", lambda en, n_=nt: en.dma_start(
                    out=x2_scr[:, :, n_ * 512:(n_ + 1) * 512], in_=x2c[:]))
                p.wt("s", sV, vx2)
                for hf in range(2):
                    if x2q8_busy[0] is not None:
                        p.wt("s", sT, x2q8_busy[0])
                    scp = p.dat("s", lambda en, h_=hf: en.activation(
                        x2q8[:], x2c[:, h_ * 8:(h_ + 1) * 8, :], AF.Square))
                    if hf == 0:
                        p.wt("t", sV, vx2)
                    for dc in range(hf * 8, hf * 8 + 8):
                        add_mm(psE[:, 0:512], ones_cf[:], x2c[:, dc, :], dc == 0, dc == DC - 1)
                    p.wt("t", sS, scp)
                    for dc8 in range(8):
                        dc = hf * 8 + dc8
                        x2q8_busy[0] = add_mm(psE[:, 512:1024], ones_c[:], x2q8[:, dc8, :],
                                              dc == 0, dc == DC - 1)
                vr2 = stats_math()
                psX = psB if nt == 0 else psA
                p.wt("t", sV, vr2)
                p.dat("t", lambda en, q=psX: en.matmul(q[:, 0:512], ones_r[:], st_r[:],
                                                       start=True, stop=True))
                tX = p.dat("t", lambda en, q=psX: en.matmul(q[:, 512:1024], ones_r[:], st_b[:],
                                                            start=True, stop=True))
                rX = bass.AP(psX, 0, [[1024, 128], [0, DC], [1, 512]])
                bX = bass.AP(psX, 512, [[1024, 128], [0, DC], [1, 512]])
                p.wt("v", sT, tX)
                p.dat("v", lambda en, n_=nt, r_=rX: en.tensor_tensor(
                    h2[:, :, n_ * 512:(n_ + 1) * 512], x2c[:], r_, ALU.mult))
                p.dat("v", lambda en, n_=nt, b_=bX: en.tensor_tensor(
                    h2[:, :, n_ * 512:(n_ + 1) * 512], h2[:, :, n_ * 512:(n_ + 1) * 512],
                    b_, ALU.add))
            p.barrier()

        es_ctxT.close()

        es_hdn = ExitStack()
        hdn = es_hdn.enter_context(nc.sbuf_tensor("hdn", [128, IB, NQ], BF16, side="right"))
        with ExitStack() as ph:
            pe = ph.enter_context
            gsb = [pe(nc.sbuf_tensor(f"gsb{i}", [128, DC, 128], BF16)) for i in range(3)]
            usb = [pe(nc.sbuf_tensor(f"usb{i}", [128, DC, 128], BF16)) for i in range(3)]
            gt_sb = [pe(nc.sbuf_tensor(f"gt{i}", [128, 1024], BF16)) for i in range(2)]

            silu_done = {}
            stt_done = {}
            mmu_done = {}
            gld = {}
            uld = {}
            for ib in range(IB):
                if ib >= 3:
                    p.wt("g", sT, mmu_done[ib - 3])
                gld[ib] = p.dma("g", f"g{ib % 3}", lambda en, i_=ib: en.dma_start(
                    out=gsb[i_ % 3][:], in_=wg_h[i_]))
                uld[ib] = p.dma("g", f"u{ib % 3}", lambda en, i_=ib: en.dma_start(
                    out=usb[i_ % 3][:], in_=wu_h[i_]))
                p.wtd("t", gld[ib])
                if ib >= 1:
                    p.wt("t", sS, silu_done[ib - 1])
                for nt in range(NT):
                    for dc in range(DC):
                        add_mm(psA[:, nt * 512:(nt + 1) * 512], gsb[ib % 3][:, dc, :],
                               h2[:, dc, nt * 512:(nt + 1) * 512], dc == 0, dc == DC - 1)
                p.wtd("t", uld[ib])
                if ib >= 1:
                    p.wt("t", sV, stt_done[ib - 1])
                for nt in range(NT):
                    for dc in range(DC):
                        mmu_done[ib] = add_mm(psB[:, nt * 512:(nt + 1) * 512],
                                              usb[ib % 3][:, dc, :],
                                              h2[:, dc, nt * 512:(nt + 1) * 512],
                                              dc == 0, dc == DC - 1)
                p.wt("s", sT, sT.n)
                if ib >= 2:
                    p.wt("s", sV, stt_done[ib - 2])
                silu_done[ib] = p.dat("s", lambda en, i_=ib: en.activation(
                    gt_sb[i_ % 2][:], psA[:], AF.Silu, scale=gs_sb[:, i_:i_ + 1]))
                p.wt("v", sS, silu_done[ib])
                stt_done[ib] = p.dat("v", lambda en, i_=ib: en.scalar_tensor_tensor(
                    hdn[:, i_, :], psB[:], us_sb[:, i_:i_ + 1], gt_sb[i_ % 2][:],
                    ALU.mult, ALU.mult))
            p.barrier()

        es_h2.close()

        with ExitStack() as ph:
            pe = ph.enter_context
            dsb = [pe(nc.sbuf_tensor(f"dsb{i}", [128, 22, 128], BF16)) for i in range(2)]
            vd = pe(nc.sbuf_tensor("vd", [128, DC, NQ], BF16))
            vdsq = pe(nc.sbuf_tensor("vdsq", [128, 512], BF16))
            x2s = pe(nc.sbuf_tensor("x2s", [128, DC, 512], F32))
            tmpf = pe(nc.sbuf_tensor("tmpf", [128, DC, 512], BF16))

            hs_mm = {}
            hs_ld = {}
            stat_done = {}
            xl0 = p.dma("y", "x2ld", lambda en: en.dma_start(
                out=x2s[:], in_=x2_scr[:, :, 0:512]))
            for oc in range(DC):
                for hb in range(2):
                    hidx = oc * 2 + hb
                    if hidx >= 2:
                        p.wt("g", sT, hs_mm[hidx - 2])
                    hs_ld[hidx] = p.dma("g", f"dsb{hidx % 2}", lambda en, o_=oc, h_=hb, sl=hidx % 2:
                                        en.dma_start(out=dsb[sl][:],
                                                     in_=wd_h[o_, :, h_ * 22:(h_ + 1) * 22, :]))
                ps = psA if oc % 2 == 0 else psB
                if oc >= 2:
                    p.wt("t", sT, stat_done[oc - 2])
                for ib in range(IB):
                    hidx = oc * 2 + ib // 22
                    if ib % 22 == 0:
                        p.wtd("t", hs_ld[hidx])
                    for nt in range(NT):
                        t_ = add_mm(ps[:, nt * 512:(nt + 1) * 512],
                                    dsb[hidx % 2][:, ib % 22, :],
                                    hdn[:, ib, nt * 512:(nt + 1) * 512],
                                    ib == 0, ib == IB - 1)
                    if ib % 22 == 21 or ib == IB - 1:
                        hs_mm[hidx] = t_
                mm_done_u = sT.n
                p.wt("v", sT, mm_done_u)
                for nt in range(NT):
                    vk = p.dat("v", lambda en, o_=oc, n_=nt, ps_=ps: en.tensor_scalar(
                        vd[:, o_, n_ * 512:(n_ + 1) * 512], ps_[:, n_ * 512:(n_ + 1) * 512],
                        ds_sb[:, o_:o_ + 1], None, ALU.mult))
                    p.wt("s", sV, vk)
                    p.wt("s", sT, sT.n)
                    sk = p.dat("s", lambda en, o_=oc, n_=nt, ps_=ps: en.activation(
                        vdsq[:], ps_[:, n_ * 512:(n_ + 1) * 512], AF.Square,
                        scale=ds_sb[:, o_:o_ + 1]))
                    p.wt("t", sS, sk)
                    s1dst = psE[:, 0:512] if nt == 0 else psC[0:1, :]
                    s2dst = psE[:, 512:1024] if nt == 0 else psD[0:1, :]
                    add_mm(s1dst, ones_c[:], vd[:, oc, nt * 512:(nt + 1) * 512],
                           oc == 0, oc == DC - 1)
                    st_ = add_mm(s2dst, ones_c[:], vdsq[:], oc == 0, oc == DC - 1)
                stat_done[oc] = st_
            stats_ready = stat_done[DC - 1]

            p.wt("v", sT, stats_ready)
            p.dat("v", lambda en: en.tensor_copy(st_ss2[:, 0:512], psC[0:1, :]))
            vsave = p.dat("v", lambda en: en.tensor_copy(st_ss2[:, 512:1024], psD[0:1, :]))

            p.wt("t", sV, vsave)
            vr0 = stats_math()
            sr0 = bcast_rb(vr0)
            p.wt("v", sT, sT.n)
            vr1 = stats_math(from_sb=st_ss2)
            p.wt("t", sV, vr1)
            p.dat("t", lambda en: en.matmul(psA[:, 0:512], ones_r[:], st_r[:], start=True, stop=True))
            tA = p.dat("t", lambda en: en.matmul(psA[:, 512:1024], ones_r[:], st_b[:], start=True, stop=True))
            p.wt("v", sS, sr0)
            p.wtd("v", xl0)
            p.dat("v", lambda en: en.tensor_tensor(
                tmpf[:], vd[:, :, 0:512], bcast_ap(rbc, DC, 512), ALU.mult))
            p.dat("v", lambda en: en.tensor_tensor(tmpf[:], tmpf[:], bcast_ap(bbc, DC, 512), ALU.add))
            v0 = p.dat("v", lambda en: en.tensor_add(x2s[:], x2s[:], tmpf[:]))
            p.wt("y", sV, v0)
            out_store = {0: p.dma("y", "outst", lambda en: en.dma_start(
                out=out_h[:, :, 0:512], in_=x2s[:]))}
            p.wtd("y", out_store[0])
            xl1 = p.dma("y", "x2ld", lambda en: en.dma_start(
                out=x2s[:], in_=x2_scr[:, :, 512:1024]))
            p.wt("v", sT, tA)
            rA = bass.AP(psA, 0, [[1024, 128], [0, DC], [1, 512]])
            bA = bass.AP(psA, 512, [[1024, 128], [0, DC], [1, 512]])
            p.dat("v", lambda en: en.tensor_tensor(tmpf[:], vd[:, :, 512:1024], rA, ALU.mult))
            p.dat("v", lambda en: en.tensor_tensor(tmpf[:], tmpf[:], bA, ALU.add))
            p.wtd("v", xl1)
            v1 = p.dat("v", lambda en: en.tensor_add(x2s[:], x2s[:], tmpf[:]))
            p.wt("y", sV, v1)
            out_store[1] = p.dma("y", "outst", lambda en: en.dma_start(
                out=out_h[:, :, 512:1024], in_=x2s[:]))
            for s_ in p.dma_sems.values():
                p.wt("y", s_, s_.n)
            p.flush()

    return nc


def _host_prep(inputs):
    x = np.asarray(inputs["L0_attn_x"], dtype=np.float32)
    Wq = np.asarray(inputs["L0_attn_Wq"], dtype=np.float32)
    Wk = np.asarray(inputs["L0_attn_Wk"], dtype=np.float32)
    Wv = np.asarray(inputs["L0_attn_Wv"], dtype=np.float32)
    Wo = np.asarray(inputs["L0_attn_Wo"], dtype=np.float32)
    gw = np.asarray(inputs["L0_ffn_gate_w"], dtype=np.float32)
    uw = np.asarray(inputs["L0_ffn_up_w"], dtype=np.float32)
    dw = np.asarray(inputs["L0_ffn_down_w"], dtype=np.float32)
    gs = np.asarray(inputs["L0_ffn_gate_s"], dtype=np.float32)
    us = np.asarray(inputs["L0_ffn_up_s"], dtype=np.float32)
    ds = np.asarray(inputs["L0_ffn_down_s"], dtype=np.float32)

    perm = np.concatenate([
        np.concatenate([h * 128 + np.arange(0, 128, 2), h * 128 + np.arange(1, 128, 2)])
        for h in range(H)])

    def wqk_t(W):
        Wp = W[perm]
        return np.ascontiguousarray(
            Wp.reshape(H, 128, DC, 128).transpose(3, 2, 0, 1))

    wq_t = wqk_t(Wq)
    wk_t = wqk_t(Wk)
    wv_t = np.ascontiguousarray(Wv.T.reshape(DC, 128, D).transpose(1, 0, 2))
    wo_t = np.ascontiguousarray(Wo.reshape(DC, 128, DC, 128).transpose(0, 3, 2, 1))
    wg_t = np.ascontiguousarray(gw.reshape(IB, 128, DC, 128).transpose(0, 3, 2, 1))
    wu_t = np.ascontiguousarray(uw.reshape(IB, 128, DC, 128).transpose(0, 3, 2, 1))
    wd_t = np.ascontiguousarray(dw.reshape(DC, 128, IB, 128).transpose(0, 3, 2, 1))
    gs_t = np.ascontiguousarray(gs.reshape(IB, 128).T)
    us_t = np.ascontiguousarray(us.reshape(IB, 128).T)
    ds_t = np.ascontiguousarray(ds.reshape(DC, 128).T)

    inv_freq = np.exp(-math.log(10000.0) * np.arange(0, DK, 2, dtype=np.float64) / DK)
    pos_all = np.arange(S, dtype=np.float64)
    ang_all = pos_all[None, :] * inv_freq[:, None]
    rkc = np.cos(ang_all).astype(np.float32)
    rks = np.sin(ang_all).astype(np.float32)

    xT = [np.ascontiguousarray(x[b].T.reshape(DC, 128, S).transpose(1, 0, 2)) for b in range(B)]

    in_maps = []
    qpos_all = []
    for core in range(8):
        b, c = core // 4, core % 4
        qpos = np.concatenate([np.arange(512 * c, 512 * c + 512),
                               np.arange(2048 + 512 * c, 2048 + 512 * c + 512)])
        qpos_all.append((b, qpos))
        xq_t = np.ascontiguousarray(x[b][qpos].T.reshape(DC, 128, NQ).transpose(1, 0, 2))
        ang_q = qpos.astype(np.float64)[None, :] * inv_freq[:, None]
        rqc = np.cos(ang_q).astype(np.float32)
        rqs = np.sin(ang_q).astype(np.float32)
        pidx = np.arange(128)[:, None, None]
        iidx = np.arange(16)[None, :, None]
        qidx = np.arange(512)[None, None, :]
        mask = ((128 * iidx + pidx) <= (512 * c + qidx)).astype(ml_dtypes.bfloat16)
        in_maps.append({
            "xq": xq_t, "xctx": xT[b],
            "wq": wq_t, "wk": wk_t, "wv": wv_t, "wo": wo_t,
            "wg": wg_t, "wu": wu_t, "wd": wd_t,
            "gs": gs_t, "us": us_t, "ds": ds_t,
            "rqc": rqc, "rqs": rqs, "rkc": rkc, "rks": rks,
            "mask": mask,
        })
    return in_maps, qpos_all


def kernel(**inputs):
    in_maps, qpos_all = _host_prep(inputs)
    nc = build_nc()
    res = run_bass_kernel_spmd(nc, in_maps, core_ids=list(range(8)))
    out = np.empty((B, S, D), dtype=np.float32)
    for core in range(8):
        b, qpos = qpos_all[core]
        o = res.results[core]["out"]
        out[b, qpos, :] = o.transpose(1, 0, 2).reshape(D, NQ).T
    return out


# version 24
# speedup vs baseline: 1.0118x; 1.0118x over previous
import math
from contextlib import ExitStack

import numpy as np
import ml_dtypes

import concourse.bass as bass
import concourse.mybir as mybir
from concourse.bass_utils import run_bass_kernel_spmd

F32 = mybir.dt.float32
BF16 = mybir.dt.bfloat16
AF = mybir.ActivationFunctionType
ALU = mybir.AluOpType

B, S, D, H, DK, I = 2, 4096, 2048, 16, 128, 5632
DC = D // 128
IB = I // 128
NQ = 1024
NT = NQ // 512
SCALE = 1.0 / math.sqrt(DK)
EPS = 1e-5
KT_A, KT_B = 16, 32


class Sem:
    def __init__(self, h):
        self.h = h
        self.n = 0

    def inc(self, k=1):
        self.n += k
        return self.n


class Prog:

    def __init__(self, nc, block):
        self.nc = nc
        self.block = block
        self.q = {k: [] for k in "tvsgy"}
        self.dsem = {}
        self.dma_sems = {}
        self.sem_alloc = None

    def dat(self, eng, f):
        s = self.dsem[eng]
        if eng in "vsg" and s.n > 0:
            self.wt(eng, s, s.n)
        tgt = s.inc(1)
        self.q[eng].append(lambda en, f_=f, h=s.h: f_(en).then_inc(h, 1))
        return tgt

    def dsema(self, key):
        if key not in self.dma_sems:
            self.dma_sems[key] = Sem(self.sem_alloc("sd_" + key))
        return self.dma_sems[key]

    def dma(self, eng, key, f):
        s = self.dsema(key)
        tgt = s.inc(16)
        self.q[eng].append(lambda en, f_=f, h=s.h: f_(en).then_inc(h, 16))
        return (s, tgt)

    def wtd(self, eng, st):
        s, t = st
        self.wt(eng, s, t)

    def wt(self, eng, sem, val):
        if val > 0:
            self.q[eng].append(lambda en, h=sem.h, v=val: en.wait_ge(h, v))

    def flush(self):
        q, self.q = self.q, {k: [] for k in "tvsgy"}
        if q["y"]:
            self.block.sync(lambda e, fs=tuple(q["y"]): [f(e) for f in fs])
        if q["g"]:
            self.block.gpsimd(lambda e, fs=tuple(q["g"]): [f(e) for f in fs])
        if q["t"]:
            self.block.tensor(lambda e, fs=tuple(q["t"]): [f(e) for f in fs])
        if q["s"]:
            self.block.scalar(lambda e, fs=tuple(q["s"]): [f(e) for f in fs])
        if q["v"]:
            self.block.vector(lambda e, fs=tuple(q["v"]): [f(e) for f in fs])

    def barrier(self):
        sems = list(self.dsem.values()) + list(self.dma_sems.values())
        for eng in "ytsvg":
            for s in sems:
                self.wt(eng, s, s.n)
        self.flush()


def bcast_ap(t, nrep, ncols):
    return bass.AP(t, 0, [[ncols, 128], [0, nrep], [1, ncols]])


def build_nc():
    nc = bass.Bass("TRN2", target_bir_lowering=False)

    xq_h = nc.dram_tensor("xq", [128, DC, NQ], F32, kind="ExternalInput").ap()
    xctx_h = nc.dram_tensor("xctx", [128, DC, S], F32, kind="ExternalInput").ap()
    wq_h = nc.dram_tensor("wq", [128, DC, H, 128], F32, kind="ExternalInput").ap()
    wk_h = nc.dram_tensor("wk", [128, DC, H, 128], F32, kind="ExternalInput").ap()
    wv_h = nc.dram_tensor("wv", [128, DC, D], F32, kind="ExternalInput").ap()
    wo_h = nc.dram_tensor("wo", [DC, 128, DC, 128], F32, kind="ExternalInput").ap()
    wg_h = nc.dram_tensor("wg", [IB, 128, DC, 128], F32, kind="ExternalInput").ap()
    wu_h = nc.dram_tensor("wu", [IB, 128, DC, 128], F32, kind="ExternalInput").ap()
    wd_h = nc.dram_tensor("wd", [DC, 128, IB, 128], F32, kind="ExternalInput").ap()
    gs_h = nc.dram_tensor("gs", [128, IB], F32, kind="ExternalInput").ap()
    us_h = nc.dram_tensor("us", [128, IB], F32, kind="ExternalInput").ap()
    ds_h = nc.dram_tensor("ds", [128, DC], F32, kind="ExternalInput").ap()
    rqc_h = nc.dram_tensor("rqc", [64, NQ], F32, kind="ExternalInput").ap()
    rqs_h = nc.dram_tensor("rqs", [64, NQ], F32, kind="ExternalInput").ap()
    rkc_h = nc.dram_tensor("rkc", [64, S], F32, kind="ExternalInput").ap()
    rks_h = nc.dram_tensor("rks", [64, S], F32, kind="ExternalInput").ap()
    mask_h = nc.dram_tensor("mask", [128, 16, 512], BF16, kind="ExternalInput").ap()
    out_h = nc.dram_tensor("out", [128, DC, NQ], F32, kind="ExternalOutput").ap()

    ln_scr = nc.dram_tensor("ln_scr", [128, DC, S + NQ], BF16).ap()
    k_scr = nc.dram_tensor("k_scr", [H, 128, S], BF16).ap()
    v_scr = nc.dram_tensor("v_scr", [128, S // 128, D], BF16).ap()
    x2_scr = nc.dram_tensor("x2_scr", [128, DC, NQ], F32).ap()

    es_top = ExitStack()
    with es_top:
        e = es_top.enter_context
        ones_c = e(nc.sbuf_tensor("ones_c", [128, 1], BF16))
        ones_cf = e(nc.sbuf_tensor("ones_cf", [128, 1], F32))
        epsb = e(nc.sbuf_tensor("epsb", [128, 1], F32))
        ones_r = e(nc.sbuf_tensor("ones_r", [1, 128], F32))
        st_ss2 = e(nc.sbuf_tensor("st_ss2", [1, 1024], F32))
        st_m = e(nc.sbuf_tensor("st_m", [1, 512], F32))
        st_q = e(nc.sbuf_tensor("st_q", [1, 512], F32))
        st_r = e(nc.sbuf_tensor("st_r", [1, 512], F32))
        st_b = e(nc.sbuf_tensor("st_b", [1, 512], F32))
        den_r = e(nc.sbuf_tensor("den_r", [1, 512], F32))
        rbc = e(nc.sbuf_tensor("rbc", [128, 512], F32))
        bbc = e(nc.sbuf_tensor("bbc", [128, 512], F32))
        gs_sb = e(nc.sbuf_tensor("gs_sb", [128, IB], F32))
        us_sb = e(nc.sbuf_tensor("us_sb", [128, IB], F32))
        ds_sb = e(nc.sbuf_tensor("ds_sb", [128, DC], F32))
        psA = e(nc.psum_tensor("psA", [128, 1024], F32))
        psB = e(nc.psum_tensor("psB", [128, 1024], F32))
        psC = e(nc.psum_tensor("psC", [128, 512], F32))
        psD = e(nc.psum_tensor("psD", [128, 512], F32))
        psE = e(nc.psum_tensor("psE", [1, 1024], F32))
        sT = Sem(e(nc.semaphore("sT")))
        sV = Sem(e(nc.semaphore("sV")))
        sS = Sem(e(nc.semaphore("sS")))
        sG = Sem(e(nc.semaphore("sG")))
        block = e(nc.Block())
        p = Prog(nc, block)
        p.dsem = {"t": sT, "v": sV, "s": sS, "g": sG}
        p.sem_alloc = lambda name: e(nc.semaphore(name))

        def add_mm(ps_ap, lhsT, rhs, start, stop):
            return p.dat("t", lambda en, o=ps_ap, l=lhsT, r=rhs, a=start, z=stop:
                         en.matmul(o, l, r, start=a, stop=z))

        def stats_math(from_sb=None):
            if from_sb is None:
                p.wt("v", sT, sT.n)
                s1ap, s2ap = psE[:, 0:512], psE[:, 512:1024]
            else:
                s1ap, s2ap = from_sb[:, 0:512], from_sb[:, 512:1024]
            p.dat("v", lambda en, a=s1ap: en.tensor_scalar_mul(st_m[:], a, 1.0 / D))
            p.dat("v", lambda en: en.scalar_tensor_tensor(st_q[:], st_m[:], 1.0, st_m[:], ALU.mult, ALU.mult))
            vq = p.dat("v", lambda en, a=s2ap: en.scalar_tensor_tensor(
                st_q[:], a, 1.0 / D, st_q[:], ALU.mult, ALU.subtract))
            p.wt("s", sV, vq)
            sq = p.dat("s", lambda en: en.activation(st_q[:], st_q[:], AF.Sqrt, bias=epsb[0:1, :]))
            p.wt("v", sS, sq)
            p.dat("v", lambda en: en.reciprocal(st_r[:], st_q[:]))
            return p.dat("v", lambda en: en.scalar_tensor_tensor(st_b[:], st_m[:], -1.0, st_r[:], ALU.mult, ALU.mult))

        def bcast_rb(v_ready):
            p.wt("t", sV, v_ready)
            p.dat("t", lambda en: en.matmul(psC[:], ones_r[:], st_r[:], start=True, stop=True))
            t2 = p.dat("t", lambda en: en.matmul(psD[:], ones_r[:], st_b[:], start=True, stop=True))
            p.wt("s", sT, t2)
            p.dat("s", lambda en: en.copy(rbc[:], psC[:]))
            return p.dat("s", lambda en: en.copy(bbc[:], psD[:]))

        def emit_ln_chunk(src_f32, dst_bf16, xb16, xsq):
            p.dat("s", lambda en, s_=src_f32: en.copy(xb16[:], s_))
            s0 = p.dat("s", lambda en, s_=src_f32: en.activation(xsq[:], s_, AF.Square))
            p.wt("t", sS, s0)
            for dc in range(DC):
                add_mm(psE[:, 0:512], ones_c[:], xb16[:, dc, :], dc == 0, dc == DC - 1)
            for dc in range(DC):
                add_mm(psE[:, 512:1024], ones_c[:], xsq[:, dc, :], dc == 0, dc == DC - 1)
            vr = stats_math()
            sr = bcast_rb(vr)
            p.wt("v", sS, sr)
            p.dat("v", lambda en, d_=dst_bf16, s_=src_f32: en.tensor_tensor(
                d_, s_, bcast_ap(rbc, DC, 512), ALU.mult))
            return p.dat("v", lambda en, d_=dst_bf16: en.tensor_tensor(
                d_, d_, bcast_ap(bbc, DC, 512), ALU.add))

        p.dat("g", lambda en: en.memset(ones_c[:], 1.0))
        p.dat("g", lambda en: en.memset(ones_cf[:], 1.0))
        p.dat("g", lambda en: en.memset(epsb[:], EPS))
        p.dat("g", lambda en: en.memset(ones_r[:], 1.0))
        p.dma("y", "const", lambda en: en.dma_start(out=gs_sb[:], in_=gs_h))
        p.dma("y", "const", lambda en: en.dma_start(out=us_sb[:], in_=us_h))
        p.dma("y", "const", lambda en: en.dma_start(out=ds_sb[:], in_=ds_h))
        p.barrier()

        with ExitStack() as ph:
            pe = ph.enter_context
            xa = [pe(nc.sbuf_tensor(f"xa{i}", [128, DC, 512], F32)) for i in range(4)]
            lno = [pe(nc.sbuf_tensor(f"lno{i}", [128, DC, 512], BF16)) for i in range(1)]
            xsq = [pe(nc.sbuf_tensor(f"xsq{i}", [128, DC, 512], BF16)) for i in range(2)]

            stats_tgt = {}
            pse_free = {}
            vr = {}
            t2b = {}
            cp_tgt = {}
            norm = {}
            store_done = {}
            sbank = [(psE[:, 0:512], psE[:, 512:1024]), (psA[0:1, 0:512], psA[0:1, 512:1024])]

            def s_prep(i):
                p.wtd("s", lds[i])
                if i >= 2:
                    p.wt("s", sT, stats_tgt[i - 2])
                s0 = p.dat("s", lambda en, b=xa[i % 4], q=xsq[i % 2]: en.activation(q[:], b[:], AF.Square))
                p.wtd("t", lds[i])
                if i >= 2:
                    p.wt("t", sV, pse_free[i - 2])
                b1, b2 = sbank[i % 2]
                for dc in range(DC):
                    add_mm(b1, ones_cf[:], xa[i % 4][:, dc, :], dc == 0, dc == DC - 1)
                p.wt("t", sS, s0)
                for dc in range(DC):
                    stats_tgt[i] = add_mm(b2, ones_c[:], xsq[i % 2][:, dc, :], dc == 0, dc == DC - 1)

            def math(i):
                b1, b2 = sbank[i % 2]
                p.wt("v", sT, stats_tgt[i])
                p.dat("v", lambda en, a=b1: en.tensor_scalar_mul(st_m[:], a, 1.0 / D))
                p.dat("v", lambda en: en.scalar_tensor_tensor(st_q[:], st_m[:], 1.0, st_m[:], ALU.mult, ALU.mult))
                pse_free[i] = p.dat("v", lambda en, a=b2: en.scalar_tensor_tensor(
                    st_q[:], a, 1.0 / D, st_q[:], ALU.mult, ALU.subtract))
                p.wt("s", sV, pse_free[i])
                sq = p.dat("s", lambda en: en.activation(st_q[:], st_q[:], AF.Sqrt, bias=epsb[0:1, :]))
                p.wt("v", sS, sq)
                if i >= 1:
                    p.wt("v", sT, t2b[i - 1])
                p.dat("v", lambda en: en.reciprocal(st_r[:], st_q[:]))
                vr[i] = p.dat("v", lambda en: en.scalar_tensor_tensor(st_b[:], st_m[:], -1.0, st_r[:], ALU.mult, ALU.mult))

            def finish(j):
                p.wt("t", sV, vr[j])
                if j >= 1:
                    p.wt("t", sS, cp_tgt[j - 1])
                p.dat("t", lambda en: en.matmul(psC[:], ones_r[:], st_r[:], start=True, stop=True))
                t2b[j] = p.dat("t", lambda en: en.matmul(psD[:], ones_r[:], st_b[:], start=True, stop=True))
                p.wt("s", sT, t2b[j])
                if j >= 1:
                    for sm_, ct_ in norm[j - 1]:
                        p.wt("s", sm_, ct_)
                p.dat("s", lambda en: en.copy(rbc[:], psC[:]))
                cp_tgt[j] = p.dat("s", lambda en: en.copy(bbc[:], psD[:]))
                eng = "v"
                p.wt(eng, sS, cp_tgt[j])
                srcb = xa[j % 4]
                if j >= 1:
                    p.wtd(eng, store_done[j - 1])
                p.dat(eng, lambda en, s_=srcb: en.tensor_tensor(
                    lno[0][:], s_[:], bcast_ap(rbc, DC, 512), ALU.mult))
                norm[j] = [(p.dsem[eng], p.dat(eng, lambda en: en.tensor_tensor(
                    lno[0][:], lno[0][:], bcast_ap(bbc, DC, 512), ALU.add)))]
                for sm_, ct_ in norm[j]:
                    p.wt("y", sm_, ct_)
                store_done[j] = p.dma("y", "lno0", lambda en, j_=j: en.dma_start(
                    out=ln_scr[:, :, j_ * 512:(j_ + 1) * 512], in_=lno[0][:]))

            lds = {}
            for i in range(10):
                if i >= 4:
                    for sm_, ct_ in norm[i - 4]:
                        p.wt("y", sm_, ct_)
                srch = (xctx_h[:, :, i * 512:(i + 1) * 512] if i < 8
                        else xq_h[:, :, (i - 8) * 512:(i - 7) * 512])
                lds[i] = p.dma("y", f"xa{i % 4}", lambda en, s=srch, b=xa[i % 4]: en.dma_start(out=b[:], in_=s))
                if i >= 2:
                    finish(i - 2)
                s_prep(i)
                if i >= 1:
                    math(i - 1)
            finish(8)
            math(9)
            finish(9)
            p.barrier()

        def emit_rope(eng, src_top, src_bot, dst_top, dst_bot, cos_ap, sin_ap, t1, t2, width):
            p.dat(eng, lambda en: en.tensor_tensor(t1[:, 0:width], src_top, cos_ap, ALU.mult))
            p.dat(eng, lambda en: en.tensor_tensor(t2[:, 0:width], src_bot, sin_ap, ALU.mult))
            p.dat(eng, lambda en: en.tensor_sub(dst_top, t1[:, 0:width], t2[:, 0:width]))
            p.dat(eng, lambda en: en.tensor_tensor(t1[:, 0:width], src_top, sin_ap, ALU.mult))
            p.dat(eng, lambda en: en.tensor_tensor(t2[:, 0:width], src_bot, cos_ap, ALU.mult))
            return (p.dsem[eng], p.dat(eng, lambda en: en.tensor_add(dst_bot, t1[:, 0:width], t2[:, 0:width])))

        with ExitStack() as ph:
            pe = ph.enter_context
            wk_sb = pe(nc.sbuf_tensor("wk_sb", [128, DC, H, 128], BF16))
            wv_sb = pe(nc.sbuf_tensor("wv_sb", [128, DC, D], BF16))
            lns = [pe(nc.sbuf_tensor(f"lns{i}", [128, DC, 512], BF16)) for i in range(2)]
            kout = [pe(nc.sbuf_tensor(f"kout{i}", [128, 512], BF16)) for i in range(4)]
            vout = [pe(nc.sbuf_tensor(f"vout{i}", [128, D], BF16)) for i in range(2)]
            rkc = [pe(nc.sbuf_tensor(f"rkc{i}", [64, 512], F32)) for i in range(2)]
            rks = [pe(nc.sbuf_tensor(f"rks{i}", [64, 512], F32)) for i in range(2)]
            tv1 = pe(nc.sbuf_tensor("tv1k", [64, 512], F32))
            tv2 = pe(nc.sbuf_tensor("tv2k", [64, 512], F32))

            kslot = [psA[:, 0:512], psA[:, 512:1024], psB[:, 0:512], psB[:, 512:1024]]
            wkl = [p.dma("g", f"wk{i}", lambda en, i_=i: en.dma_start(
                out=wk_sb[:, :, 4 * i_:4 * i_ + 4, :], in_=wk_h[:, :, 4 * i_:4 * i_ + 4, :]))
                   for i in range(4)]
            wvl = [p.dma("g", f"wv{i}", lambda en, i_=i: en.dma_start(
                out=wv_sb[:, :, 512 * i_:512 * (i_ + 1)], in_=wv_h[:, :, 512 * i_:512 * (i_ + 1)]))
                   for i in range(4)]
            ku = 0
            gidx = 0
            vu = 0
            rope_done = {}
            kst_done = {}
            ev_list = []
            vst_done = {}
            slice_mm_done = {}
            rop_ld = {}
            for s in range(8):
                if s >= 2:
                    p.wt("y", sT, slice_mm_done[s - 2])
                ld = p.dma("y", f"lnsk{s % 2}", lambda en, s_=s, b=lns[s % 2]: en.dma_start(
                    out=b[:], in_=ln_scr[:, :, s_ * 512:(s_ + 1) * 512]))
                if s >= 2:
                    p.wt("y", *rope_done.get((s - 2) * 16 + 15, (sV, 0)))
                p.dma("y", f"rk{s % 2}", lambda en, s_=s, b=rkc[s % 2]: en.dma_start(
                    out=b[:], in_=rkc_h[:, s_ * 512:(s_ + 1) * 512]))
                rop_ld[s] = p.dma("y", f"rk{s % 2}", lambda en, s_=s, b=rks[s % 2]: en.dma_start(
                    out=b[:], in_=rks_h[:, s_ * 512:(s_ + 1) * 512]))
                first = True

                def k_unit(h, s_=s):
                    nonlocal ku, first
                    ps = kslot[ku % 4]
                    if ku < 16:
                        p.wtd("t", wkl[h // 4])
                    if ku >= 4:
                        p.wt("t", *rope_done[ku - 4])
                    if first:
                        p.wtd("t", ld)
                        first = False
                    for dc in range(DC):
                        add_mm(ps, wk_sb[:, dc, h, :], lns[s_ % 2][:, dc, :],
                               dc == 0, dc == DC - 1)
                    mmt = sT.n
                    ko = kout[ku % 4]
                    p.wt("v", sT, mmt)
                    if ku == 0 or (ku % 16 == 0):
                        p.wtd("v", rop_ld[s_])
                    if ku >= 4:
                        p.wtd("v", kst_done[ku - 4])
                    rope_done[ku] = emit_rope("v", ps[0:64, :], ps[64:128, :],
                                              ko[0:64, :], ko[64:128, :],
                                              rkc[s_ % 2][:, :], rks[s_ % 2][:, :],
                                              tv1, tv2, 512)
                    p.wt("y", *rope_done[ku])
                    kst_done[ku] = p.dma("y", f"kout{ku % 4}", lambda en, h_=h, s2=s_, b=ko:
                                         en.dma_start(out=k_scr[h_, :, s2 * 512:(s2 + 1) * 512], in_=b[:]))
                    ku += 1

                def v_unit(tt, s_=s):
                    nonlocal vu, gidx, first
                    vo = vout[vu % 2]
                    for dvc in range(4):
                        ps = psC if gidx % 2 == 0 else psD
                        if vu < 4:
                            p.wtd("t", wvl[dvc])
                        if first:
                            p.wtd("t", ld)
                            first = False
                        if gidx >= 2:
                            p.wt("t", sS, ev_list[gidx - 2])
                        for dc in range(DC):
                            add_mm(ps[:], lns[s_ % 2][:, dc, tt * 128:(tt + 1) * 128],
                                   wv_sb[:, dc, dvc * 512:(dvc + 1) * 512],
                                   dc == 0, dc == DC - 1)
                        p.wt("s", sT, sT.n)
                        if vu >= 2 and dvc == 0:
                            p.wtd("s", vst_done[vu - 2])
                        ev_list.append(p.dat("s", lambda en, v_=dvc, ps_=ps, b=vo: en.copy(
                            b[:, v_ * 512:(v_ + 1) * 512], ps_[:])))
                        gidx += 1
                    p.wt("y", sS, ev_list[-1])
                    vst_done[vu] = p.dma("y", f"vout{vu % 2}", lambda en, s2=s_, t_=tt, b=vo:
                                         en.dma_start(out=v_scr[:, s2 * 4 + t_, :], in_=b[:]))
                    vu += 1

                if s == 0:
                    for h in range(H):
                        k_unit(h)
                    for blk in range(4):
                        v_unit(blk)
                else:
                    for blk in range(4):
                        for h in range(4 * blk, 4 * blk + 4):
                            k_unit(h)
                        v_unit(blk)
                slice_mm_done[s] = sT.n
            p.barrier()

        es_qT = ExitStack()
        qT = es_qT.enter_context(nc.sbuf_tensor("qT", [128, H, NQ], BF16))

        with ExitStack() as ph:
            pe = ph.enter_context
            wq_sb = pe(nc.sbuf_tensor("wq_sb", [128, DC, H, 128], BF16))
            lnq = [pe(nc.sbuf_tensor(f"lnq{i}", [128, DC, 512], BF16)) for i in range(2)]
            rqc = pe(nc.sbuf_tensor("rqc_s", [64, NQ], F32))
            rqs = pe(nc.sbuf_tensor("rqs_s", [64, NQ], F32))
            tv1 = pe(nc.sbuf_tensor("tv1", [64, 1024], F32))
            tv2 = pe(nc.sbuf_tensor("tv2", [64, 1024], F32))

            wql = [p.dma("g", f"wq{i}", lambda en, i_=i: en.dma_start(
                out=wq_sb[:, :, 4 * i_:4 * i_ + 4, :], in_=wq_h[:, :, 4 * i_:4 * i_ + 4, :]))
                   for i in range(4)]
            p.dma("y", "const", lambda en: en.dma_start(out=rqc[:], in_=rqc_h))
            rl = p.dma("y", "const", lambda en: en.dma_start(out=rqs[:], in_=rqs_h))
            lql = [p.dma("y", f"lnq{i}", lambda en, i_=i: en.dma_start(
                out=lnq[i_][:], in_=ln_scr[:, :, S + i_ * 512:S + (i_ + 1) * 512]))
                for i in range(2)]
            p.wtd("t", lql[0])
            p.wtd("t", lql[1])
            p.wtd("v", rl)
            rope_done = {}
            for h in range(H):
                ps = psA if h % 2 == 0 else psB
                p.wtd("t", wql[h // 4])
                if h >= 2:
                    p.wt("t", *rope_done[h - 2])
                for nt in range(NT):
                    for dc in range(DC):
                        add_mm(ps[:, nt * 512:(nt + 1) * 512], wq_sb[:, dc, h, :],
                               lnq[nt][:, dc, :], dc == 0, dc == DC - 1)
                p.wt("v", sT, sT.n)
                rope_done[h] = emit_rope("v", ps[0:64, 0:1024], ps[64:128, 0:1024],
                                         qT[0:64, h, :], qT[64:128, h, :],
                                         rqc[:, 0:1024], rqs[:, 0:1024], tv1, tv2, 1024)
            p.barrier()

        es_ctxT = ExitStack()
        ctxT = es_ctxT.enter_context(nc.sbuf_tensor("ctxT", [128, H, NQ], BF16, side="right"))
        with ExitStack() as ph:
            pe = ph.enter_context
            mask_sb = pe(nc.sbuf_tensor("mask_sb", [128, 16, 512], BF16))
            k_sb = [pe(nc.sbuf_tensor(f"ksb{i}", [128, S], BF16)) for i in range(2)]
            v_sb = [pe(nc.sbuf_tensor(f"vsb{i}", [128, S // 128, 256], BF16)) for i in range(2)]
            e_sb = [[pe(nc.sbuf_tensor(f"esb{s}{i}", [128, 1024], BF16)) for i in range(2)]
                    for s in range(2)]
            den_rs = [pe(nc.sbuf_tensor(f"denr{s}", [1, 512], F32)) for s in range(2)]

            ml = p.dma("y", "const", lambda en: en.dma_start(out=mask_sb[:], in_=mask_h))
            p.wtd("g", ml)
            p.wtd("v", ml)

            KT = [KT_A, KT_B]
            NP = [KT_A // 2, KT_B // 2]
            ctx_ps = [psC, psD]
            den_ap = [psE[:, 0:512], psE[:, 512:1024]]
            pslot = [psA, psB]
            pslot_free = [None, None]
            pair_ctr = 0
            bundle = 0
            pend = []
            hd = {}
            rbc_guard = [0]
            mask_eng = [0]
            vsb_ld = [None, None]
            prev_at = [(0, 0), (0, 0)]

            def flush_pend(force=False):
                while pend and (force or bundle >= pend[0]["after"]):
                    ev = pend.pop(0)
                    s_, h_, sl = ev["s"], ev["h"], ev["slot"]
                    p.wt("t", sV, ev["vr"])
                    if pslot_free[sl] is not None:
                        p.wt("t", *pslot_free[sl])
                    tb = p.dat("t", lambda en, ps_=pslot[sl], d=den_rs[s_]: en.matmul(
                        ps_[:, 0:512], ones_r[:], d[:], start=True, stop=True))
                    p.wt("s", sT, tb)
                    if rbc_guard[0]:
                        p.wt("s", sV, rbc_guard[0])
                    rc = p.dat("s", lambda en, ps_=pslot[sl]: en.copy(rbc[:], ps_[:, 0:512]))
                    pslot_free[sl] = (sS, rc)
                    p.wt("v", sS, rc)
                    hd[(h_, s_)] = p.dat("v", lambda en, h2=h_, s2=s_, cp=ctx_ps[s_]: en.tensor_mul(
                        ctxT[:, h2, s2 * 512:(s2 + 1) * 512], cp[:], rbc[:]))
                    rbc_guard[0] = hd[(h_, s_)]

            for h in range(H):
                if h % 2 == 0:
                    if h >= 4:
                        p.wt("y", sV, hd[(h - 3, 1)])
                    vsb_ld[(h // 2) % 2] = p.dma("y", f"vsb{(h // 2) % 2}",
                        lambda en, h_=h, b=v_sb[(h // 2) % 2]: en.dma_start(
                            out=b[:], in_=v_scr[:, :, h_ * 128:(h_ + 2) * 128]))
                if h >= 2:
                    p.wt("y", sV, hd[(h - 2, 1)])
                kv_ld = p.dma("y", f"ksb{h % 2}",
                              lambda en, h_=h, b=k_sb[h % 2]: en.dma_start(
                                  out=b[:], in_=k_scr[h_]))

                et = [{}, {}]
                gt_ = [{}, {}]
                at = [{}, {}]
                last_slot = [None, None]
                firsts = [True]

                def sc_pair(s, pr, h_=h, kvl=kv_ld):
                    nonlocal pair_ctr
                    sl = pair_ctr % 2
                    pair_ctr += 1
                    if pslot_free[sl] is not None:
                        p.wt("t", *pslot_free[sl])
                    if firsts[0]:
                        p.wtd("t", kvl)
                        p.wtd("t", vsb_ld[(h_ // 2) % 2])
                        firsts[0] = False
                    for half in (0, 1):
                        t = 2 * pr + half
                        tgt = add_mm(pslot[sl][:, half * 512:(half + 1) * 512],
                                     k_sb[h_ % 2][:, t * 128:(t + 1) * 128],
                                     qT[:, h_, s * 512:(s + 1) * 512], True, True)
                    last_slot[s] = sl
                    return sl, tgt

                def exp_pair(s, pr, sl, sc):
                    p.wt("s", sT, sc)
                    if pr >= 2:
                        p.wt("s", sT, at[s][pr - 2])
                    elif prev_at[s][pr]:
                        p.wt("s", sT, prev_at[s][pr])
                    eb = e_sb[s][pr % 2]
                    et[s][pr] = p.dat("s", lambda en, b=eb, ps_=pslot[sl]:
                                      en.activation(b[:], ps_[:], AF.Exp, scale=SCALE))
                    pslot_free[sl] = (sS, et[s][pr])
                    masked = (s == 0) or (pr >= 8)
                    if masked:
                        mi = 2 * (pr if s == 0 else pr - 8)
                        eng = "v" if mask_eng[0] == 0 else "g"
                        mask_eng[0] ^= 1
                        p.wt(eng, sS, et[s][pr])
                        mask_ap = bass.AP(mask_sb, mi * 512, [[16 * 512, 128], [1, 1024]])
                        gt_[s][pr] = (p.dsem[eng], p.dat(eng, lambda en, b=eb, m=mask_ap:
                                      en.tensor_mul(b[:], b[:], m)))
                    else:
                        gt_[s][pr] = (sS, et[s][pr])

                def av_pair(s, pr, h_=h):
                    gsem, gv = gt_[s][pr]
                    p.wt("t", gsem, gv)
                    if pr == 0 and h_ >= 1:
                        p.wt("t", sV, hd[(h_ - 1, s)])
                    eb = e_sb[s][pr % 2]
                    for half in (0, 1):
                        t = 2 * pr + half
                        vap = bass.AP(v_sb[(h_ // 2) % 2], t * 256 + (h_ % 2) * 128,
                                      [[(S // 128) * 256, 128], [1, 128]])
                        add_mm(ctx_ps[s][:], vap, eb[:, half * 512:(half + 1) * 512],
                               t == 0, t == KT[s] - 1)
                    for half in (0, 1):
                        t = 2 * pr + half
                        at[s][pr] = add_mm(den_ap[s], ones_c[:],
                                           eb[:, half * 512:(half + 1) * 512],
                                           t == 0, t == KT[s] - 1)

                def end_stream(s, h_=h):
                    p.wt("v", sT, at[s][NP[s] - 1])
                    vr = p.dat("v", lambda en, s_=s: en.reciprocal(den_rs[s_][:], den_ap[s_]))
                    pend.append({"s": s, "h": h_, "vr": vr,
                                 "slot": last_slot[s], "after": bundle + 2})

                order = []
                for i in range(8):
                    order += [(1, 2 * i), (0, i), (1, 2 * i + 1)]
                pnd = [None, None]
                for (s, pr) in order:
                    flush_pend()
                    sl, sc = sc_pair(s, pr)
                    exp_pair(s, pr, sl, sc)
                    if pnd[s] is not None:
                        av_pair(s, pnd[s])
                    pnd[s] = pr
                    if pr == NP[s] - 1:
                        av_pair(s, pr)
                        pnd[s] = None
                        end_stream(s)
                    bundle += 1
                prev_at[0] = (at[0].get(NP[0] - 2, 0), at[0].get(NP[0] - 1, 0))
                prev_at[1] = (at[1].get(NP[1] - 2, 0), at[1].get(NP[1] - 1, 0))
            flush_pend(force=True)
            p.barrier()

        es_qT.close()

        es_h2 = ExitStack()
        h2 = es_h2.enter_context(nc.sbuf_tensor("h2", [128, DC, NQ], BF16))
        with ExitStack() as ph:
            pe = ph.enter_context
            wos = [pe(nc.sbuf_tensor(f"wos{i}", [128, DC, 128], BF16)) for i in range(2)]
            v4 = pe(nc.sbuf_tensor("v4", [128, DC, 1024], BF16))
            vsq4 = pe(nc.sbuf_tensor("vsq4", [128, 1024], BF16))
            xqs = pe(nc.sbuf_tensor("xqs", [128, DC, 512], F32))
            x2c = pe(nc.sbuf_tensor("x2c", [128, DC, 512], F32))
            x2q8 = pe(nc.sbuf_tensor("x2q8", [128, 8, 512], BF16))

            evac_done = {}
            stat_done = {}
            mm_done = {}

            def wo_stats(j):
                p.wt("t", sS, evac_done[j])
                for nt in range(NT):
                    s1dst = psE[:, 0:512] if nt == 0 else psC[0:1, :]
                    s2dst = psE[:, 512:1024] if nt == 0 else psD[0:1, :]
                    add_mm(s1dst, ones_c[:], v4[:, j, nt * 512:(nt + 1) * 512],
                           j == 0, j == DC - 1)
                    stat_done[j] = add_mm(s2dst, ones_c[:],
                                          vsq4[:, nt * 512:(nt + 1) * 512],
                                          j == 0, j == DC - 1)

            for oc in range(DC):
                if oc >= 2:
                    p.wt("g", sS, evac_done[oc - 2])
                wl = p.dma("g", f"wos{oc % 2}", lambda en, o_=oc: en.dma_start(
                    out=wos[o_ % 2][:], in_=wo_h[o_]))
                p.wtd("t", wl)
                ps = psA if oc % 2 == 0 else psB
                if oc >= 2:
                    p.wt("t", sS, evac_done[oc - 2])
                for nt in range(NT):
                    for dvc in range(DC):
                        mm_done[oc] = add_mm(ps[:, nt * 512:(nt + 1) * 512],
                                             wos[oc % 2][:, dvc, :],
                                             ctxT[:, dvc, nt * 512:(nt + 1) * 512],
                                             dvc == 0, dvc == DC - 1)
                if oc >= 1:
                    wo_stats(oc - 1)
                    p.wt("s", sT, stat_done[oc - 1])
                else:
                    p.wt("s", sT, mm_done[oc])
                p.dat("s", lambda en, o_=oc, ps_=ps: en.copy(v4[:, o_, :], ps_[:]))
                evac_done[oc] = p.dat("s", lambda en, ps_=ps: en.activation(
                    vsq4[:], ps_[:], AF.Square))
            wo_stats(DC - 1)
            stats_ready = stat_done[DC - 1]

            p.wt("v", sT, stats_ready)
            p.dat("v", lambda en: en.tensor_copy(st_ss2[:, 0:512], psC[0:1, :]))
            vsave = p.dat("v", lambda en: en.tensor_copy(st_ss2[:, 512:1024], psD[0:1, :]))

            x2_store = {}
            x2q8_busy = [None]
            for nt in range(NT):
                if nt == 1:
                    p.wt("y", sV, sV.n)
                xl = p.dma("y", "xqld", lambda en, n_=nt: en.dma_start(
                    out=xqs[:], in_=xq_h[:, :, n_ * 512:(n_ + 1) * 512]))
                if nt == 0:
                    p.wt("t", sV, vsave)
                    vr = stats_math()
                else:
                    vr = stats_math(from_sb=st_ss2)
                sr = bcast_rb(vr)
                p.wt("v", sS, sr)
                p.wtd("v", xl)
                if nt == 1:
                    p.wtd("v", x2_store[0])
                p.dat("v", lambda en, n_=nt: en.tensor_tensor(
                    x2c[:], v4[:, :, n_ * 512:(n_ + 1) * 512], bcast_ap(rbc, DC, 512), ALU.mult))
                p.dat("v", lambda en: en.tensor_tensor(x2c[:], x2c[:], bcast_ap(bbc, DC, 512), ALU.add))
                vx2 = p.dat("v", lambda en: en.tensor_add(x2c[:], x2c[:], xqs[:]))
                p.wt("y", sV, vx2)
                x2_store[nt] = p.dma("y", "x2st", lambda en, n_=nt: en.dma_start(
                    out=x2_scr[:, :, n_ * 512:(n_ + 1) * 512], in_=x2c[:]))
                p.wt("s", sV, vx2)
                for hf in range(2):
                    if x2q8_busy[0] is not None:
                        p.wt("s", sT, x2q8_busy[0])
                    scp = p.dat("s", lambda en, h_=hf: en.activation(
                        x2q8[:], x2c[:, h_ * 8:(h_ + 1) * 8, :], AF.Square))
                    if hf == 0:
                        p.wt("t", sV, vx2)
                    for dc in range(hf * 8, hf * 8 + 8):
                        add_mm(psE[:, 0:512], ones_cf[:], x2c[:, dc, :], dc == 0, dc == DC - 1)
                    p.wt("t", sS, scp)
                    for dc8 in range(8):
                        dc = hf * 8 + dc8
                        x2q8_busy[0] = add_mm(psE[:, 512:1024], ones_c[:], x2q8[:, dc8, :],
                                              dc == 0, dc == DC - 1)
                vr2 = stats_math()
                sr2 = bcast_rb(vr2)
                p.wt("v", sS, sr2)
                p.dat("v", lambda en, n_=nt: en.tensor_tensor(
                    h2[:, :, n_ * 512:(n_ + 1) * 512], x2c[:], bcast_ap(rbc, DC, 512), ALU.mult))
                p.dat("v", lambda en, n_=nt: en.tensor_tensor(
                    h2[:, :, n_ * 512:(n_ + 1) * 512], h2[:, :, n_ * 512:(n_ + 1) * 512],
                    bcast_ap(bbc, DC, 512), ALU.add))
            p.barrier()

        es_ctxT.close()

        es_hdn = ExitStack()
        hdn = es_hdn.enter_context(nc.sbuf_tensor("hdn", [128, IB, NQ], BF16, side="right"))
        with ExitStack() as ph:
            pe = ph.enter_context
            gsb = [pe(nc.sbuf_tensor(f"gsb{i}", [128, DC, 128], BF16)) for i in range(3)]
            usb = [pe(nc.sbuf_tensor(f"usb{i}", [128, DC, 128], BF16)) for i in range(3)]
            gt_sb = [pe(nc.sbuf_tensor(f"gt{i}", [128, 1024], BF16)) for i in range(2)]

            silu_done = {}
            stt_done = {}
            mmu_done = {}
            gld = {}
            uld = {}
            for ib in range(IB):
                if ib >= 3:
                    p.wt("g", sT, mmu_done[ib - 3])
                gld[ib] = p.dma("g", f"g{ib % 3}", lambda en, i_=ib: en.dma_start(
                    out=gsb[i_ % 3][:], in_=wg_h[i_]))
                uld[ib] = p.dma("g", f"u{ib % 3}", lambda en, i_=ib: en.dma_start(
                    out=usb[i_ % 3][:], in_=wu_h[i_]))
                p.wtd("t", gld[ib])
                if ib >= 1:
                    p.wt("t", sS, silu_done[ib - 1])
                for nt in range(NT):
                    for dc in range(DC):
                        add_mm(psA[:, nt * 512:(nt + 1) * 512], gsb[ib % 3][:, dc, :],
                               h2[:, dc, nt * 512:(nt + 1) * 512], dc == 0, dc == DC - 1)
                p.wtd("t", uld[ib])
                if ib >= 1:
                    p.wt("t", sV, stt_done[ib - 1])
                for nt in range(NT):
                    for dc in range(DC):
                        mmu_done[ib] = add_mm(psB[:, nt * 512:(nt + 1) * 512],
                                              usb[ib % 3][:, dc, :],
                                              h2[:, dc, nt * 512:(nt + 1) * 512],
                                              dc == 0, dc == DC - 1)
                p.wt("s", sT, sT.n)
                if ib >= 2:
                    p.wt("s", sV, stt_done[ib - 2])
                silu_done[ib] = p.dat("s", lambda en, i_=ib: en.activation(
                    gt_sb[i_ % 2][:], psA[:], AF.Silu, scale=gs_sb[:, i_:i_ + 1]))
                p.wt("v", sS, silu_done[ib])
                stt_done[ib] = p.dat("v", lambda en, i_=ib: en.scalar_tensor_tensor(
                    hdn[:, i_, :], psB[:], us_sb[:, i_:i_ + 1], gt_sb[i_ % 2][:],
                    ALU.mult, ALU.mult))
            p.barrier()

        es_h2.close()

        with ExitStack() as ph:
            pe = ph.enter_context
            dsb = [pe(nc.sbuf_tensor(f"dsb{i}", [128, 22, 128], BF16)) for i in range(2)]
            vd = pe(nc.sbuf_tensor("vd", [128, DC, NQ], BF16))
            vdsq = pe(nc.sbuf_tensor("vdsq", [128, 512], BF16))
            x2s = pe(nc.sbuf_tensor("x2s", [128, DC, 512], F32))
            tmpf = pe(nc.sbuf_tensor("tmpf", [128, DC, 512], BF16))

            hs_mm = {}
            hs_ld = {}
            stat_done = {}
            xl0 = p.dma("y", "x2ld", lambda en: en.dma_start(
                out=x2s[:], in_=x2_scr[:, :, 0:512]))
            for oc in range(DC):
                for hb in range(2):
                    hidx = oc * 2 + hb
                    if hidx >= 2:
                        p.wt("g", sT, hs_mm[hidx - 2])
                    hs_ld[hidx] = p.dma("g", f"dsb{hidx % 2}", lambda en, o_=oc, h_=hb, sl=hidx % 2:
                                        en.dma_start(out=dsb[sl][:],
                                                     in_=wd_h[o_, :, h_ * 22:(h_ + 1) * 22, :]))
                ps = psA if oc % 2 == 0 else psB
                if oc >= 2:
                    p.wt("t", sT, stat_done[oc - 2])
                for ib in range(IB):
                    hidx = oc * 2 + ib // 22
                    if ib % 22 == 0:
                        p.wtd("t", hs_ld[hidx])
                    for nt in range(NT):
                        t_ = add_mm(ps[:, nt * 512:(nt + 1) * 512],
                                    dsb[hidx % 2][:, ib % 22, :],
                                    hdn[:, ib, nt * 512:(nt + 1) * 512],
                                    ib == 0, ib == IB - 1)
                    if ib % 22 == 21 or ib == IB - 1:
                        hs_mm[hidx] = t_
                mm_done_u = sT.n
                p.wt("v", sT, mm_done_u)
                for nt in range(NT):
                    vk = p.dat("v", lambda en, o_=oc, n_=nt, ps_=ps: en.tensor_scalar(
                        vd[:, o_, n_ * 512:(n_ + 1) * 512], ps_[:, n_ * 512:(n_ + 1) * 512],
                        ds_sb[:, o_:o_ + 1], None, ALU.mult))
                    p.wt("s", sV, vk)
                    p.wt("s", sT, sT.n)
                    sk = p.dat("s", lambda en, o_=oc, n_=nt, ps_=ps: en.activation(
                        vdsq[:], ps_[:, n_ * 512:(n_ + 1) * 512], AF.Square,
                        scale=ds_sb[:, o_:o_ + 1]))
                    p.wt("t", sS, sk)
                    s1dst = psE[:, 0:512] if nt == 0 else psC[0:1, :]
                    s2dst = psE[:, 512:1024] if nt == 0 else psD[0:1, :]
                    add_mm(s1dst, ones_c[:], vd[:, oc, nt * 512:(nt + 1) * 512],
                           oc == 0, oc == DC - 1)
                    st_ = add_mm(s2dst, ones_c[:], vdsq[:], oc == 0, oc == DC - 1)
                stat_done[oc] = st_
            stats_ready = stat_done[DC - 1]

            p.wt("v", sT, stats_ready)
            p.dat("v", lambda en: en.tensor_copy(st_ss2[:, 0:512], psC[0:1, :]))
            vsave = p.dat("v", lambda en: en.tensor_copy(st_ss2[:, 512:1024], psD[0:1, :]))

            p.wt("t", sV, vsave)
            vr0 = stats_math()
            sr0 = bcast_rb(vr0)
            p.wt("v", sT, sT.n)
            vr1 = stats_math(from_sb=st_ss2)
            p.wt("t", sV, vr1)
            p.dat("t", lambda en: en.matmul(psA[:, 0:512], ones_r[:], st_r[:], start=True, stop=True))
            tA = p.dat("t", lambda en: en.matmul(psA[:, 512:1024], ones_r[:], st_b[:], start=True, stop=True))
            p.wt("v", sS, sr0)
            p.wtd("v", xl0)
            p.dat("v", lambda en: en.tensor_tensor(
                tmpf[:], vd[:, :, 0:512], bcast_ap(rbc, DC, 512), ALU.mult))
            p.dat("v", lambda en: en.tensor_tensor(tmpf[:], tmpf[:], bcast_ap(bbc, DC, 512), ALU.add))
            v0 = p.dat("v", lambda en: en.tensor_add(x2s[:], x2s[:], tmpf[:]))
            p.wt("y", sV, v0)
            out_store = {0: p.dma("y", "outst", lambda en: en.dma_start(
                out=out_h[:, :, 0:512], in_=x2s[:]))}
            p.wtd("y", out_store[0])
            xl1 = p.dma("y", "x2ld", lambda en: en.dma_start(
                out=x2s[:], in_=x2_scr[:, :, 512:1024]))
            p.wt("v", sT, tA)
            rA = bass.AP(psA, 0, [[1024, 128], [0, DC], [1, 512]])
            bA = bass.AP(psA, 512, [[1024, 128], [0, DC], [1, 512]])
            p.dat("v", lambda en: en.tensor_tensor(tmpf[:], vd[:, :, 512:1024], rA, ALU.mult))
            p.dat("v", lambda en: en.tensor_tensor(tmpf[:], tmpf[:], bA, ALU.add))
            p.wtd("v", xl1)
            v1 = p.dat("v", lambda en: en.tensor_add(x2s[:], x2s[:], tmpf[:]))
            p.wt("y", sV, v1)
            out_store[1] = p.dma("y", "outst", lambda en: en.dma_start(
                out=out_h[:, :, 512:1024], in_=x2s[:]))
            for s_ in p.dma_sems.values():
                p.wt("y", s_, s_.n)
            p.flush()

    return nc


def _host_prep(inputs):
    x = np.asarray(inputs["L0_attn_x"], dtype=np.float32)
    Wq = np.asarray(inputs["L0_attn_Wq"], dtype=np.float32)
    Wk = np.asarray(inputs["L0_attn_Wk"], dtype=np.float32)
    Wv = np.asarray(inputs["L0_attn_Wv"], dtype=np.float32)
    Wo = np.asarray(inputs["L0_attn_Wo"], dtype=np.float32)
    gw = np.asarray(inputs["L0_ffn_gate_w"], dtype=np.float32)
    uw = np.asarray(inputs["L0_ffn_up_w"], dtype=np.float32)
    dw = np.asarray(inputs["L0_ffn_down_w"], dtype=np.float32)
    gs = np.asarray(inputs["L0_ffn_gate_s"], dtype=np.float32)
    us = np.asarray(inputs["L0_ffn_up_s"], dtype=np.float32)
    ds = np.asarray(inputs["L0_ffn_down_s"], dtype=np.float32)

    perm = np.concatenate([
        np.concatenate([h * 128 + np.arange(0, 128, 2), h * 128 + np.arange(1, 128, 2)])
        for h in range(H)])

    def wqk_t(W):
        Wp = W[perm]
        return np.ascontiguousarray(
            Wp.reshape(H, 128, DC, 128).transpose(3, 2, 0, 1))

    wq_t = wqk_t(Wq)
    wk_t = wqk_t(Wk)
    wv_t = np.ascontiguousarray(Wv.T.reshape(DC, 128, D).transpose(1, 0, 2))
    wo_t = np.ascontiguousarray(Wo.reshape(DC, 128, DC, 128).transpose(0, 3, 2, 1))
    wg_t = np.ascontiguousarray(gw.reshape(IB, 128, DC, 128).transpose(0, 3, 2, 1))
    wu_t = np.ascontiguousarray(uw.reshape(IB, 128, DC, 128).transpose(0, 3, 2, 1))
    wd_t = np.ascontiguousarray(dw.reshape(DC, 128, IB, 128).transpose(0, 3, 2, 1))
    gs_t = np.ascontiguousarray(gs.reshape(IB, 128).T)
    us_t = np.ascontiguousarray(us.reshape(IB, 128).T)
    ds_t = np.ascontiguousarray(ds.reshape(DC, 128).T)

    import jax
    import jax.numpy as jnp
    with jax.default_device(jax.devices("cpu")[0]):
        _inv = jnp.exp(-math.log(10000.0) * jnp.arange(0, DK, 2, dtype=jnp.float32) / DK)
        _pos = jnp.arange(S, dtype=jnp.float32)
        _fr = _pos[:, None] * _inv[None, :]
        cos_all = np.asarray(jnp.cos(_fr))
        sin_all = np.asarray(jnp.sin(_fr))
    rkc = np.ascontiguousarray(cos_all.T)
    rks = np.ascontiguousarray(sin_all.T)

    xT = [np.ascontiguousarray(x[b].T.reshape(DC, 128, S).transpose(1, 0, 2)) for b in range(B)]

    in_maps = []
    qpos_all = []
    for core in range(8):
        b, c = core // 4, core % 4
        qpos = np.concatenate([np.arange(512 * c, 512 * c + 512),
                               np.arange(2048 + 512 * c, 2048 + 512 * c + 512)])
        qpos_all.append((b, qpos))
        xq_t = np.ascontiguousarray(x[b][qpos].T.reshape(DC, 128, NQ).transpose(1, 0, 2))
        rqc = np.ascontiguousarray(cos_all[qpos].T)
        rqs = np.ascontiguousarray(sin_all[qpos].T)
        pidx = np.arange(128)[:, None, None]
        iidx = np.arange(16)[None, :, None]
        qidx = np.arange(512)[None, None, :]
        mask = ((128 * iidx + pidx) <= (512 * c + qidx)).astype(ml_dtypes.bfloat16)
        in_maps.append({
            "xq": xq_t, "xctx": xT[b],
            "wq": wq_t, "wk": wk_t, "wv": wv_t, "wo": wo_t,
            "wg": wg_t, "wu": wu_t, "wd": wd_t,
            "gs": gs_t, "us": us_t, "ds": ds_t,
            "rqc": rqc, "rqs": rqs, "rkc": rkc, "rks": rks,
            "mask": mask,
        })
    return in_maps, qpos_all


def kernel(**inputs):
    in_maps, qpos_all = _host_prep(inputs)
    nc = build_nc()
    res = run_bass_kernel_spmd(nc, in_maps, core_ids=list(range(8)))
    out = np.empty((B, S, D), dtype=np.float32)
    for core in range(8):
        b, qpos = qpos_all[core]
        o = res.results[core]["out"]
        out[b, qpos, :] = o.transpose(1, 0, 2).reshape(D, NQ).T
    return out
